# version 01
# baseline (speedup 1.0000x reference)
import os
import numpy as np
import ml_dtypes
import concourse.bass as bass
import concourse.mybir as mybir
from concourse.bass_utils import run_bass_kernel_spmd
from contextlib import ExitStack

F32 = mybir.dt.float32
BF16 = mybir.dt.bfloat16
AF = mybir.ActivationFunctionType
ALU = mybir.AluOpType
AX = mybir.AxisListType.X

E, H, D, B, N, CHUNK = 1024, 16, 64, 2, 8192, 128
T, C, KC, IC, I = 2048, 16, 8, 24, 3072
MAGIC = 12582912.0
QMAX_A = 32767.0
QMAX_W = 7.0
EPS = 1e-5
N_CORES = 8
ENGINES = ("sync", "gpsimd", "vector", "scalar", "tensor")


def _slopes():
    def po2(m):
        start = 2.0 ** (-(2.0 ** (-(np.log2(m) - 3))))
        return [start * start ** i for i in range(m)]
    s = np.array(po2(H), dtype=np.float32)
    return (s * np.float32(1.0 + 1e-05)).astype(np.float32)


def _host_consts(core):
    s = _slopes()
    idx = np.arange(CHUNK, dtype=np.float32)
    diff = idx[:, None] - idx[None, :]
    with np.errstate(over="ignore"):
        mask = np.where(diff >= 0, np.exp(-s[:, None, None] * diff), 0.0).astype(np.float32)
    lam_q = np.exp(-s[:, None] * (idx + 1.0)).astype(np.float32)
    lam_k = np.exp(-s[:, None] * (CHUNK - 1.0 - idx)).astype(np.float32)

    order = []
    for hg in range(4):
        order += [hg * 4 + 0, hg * 4 + 2, hg * 4 + 1, hg * 4 + 3]
    maskT = np.ascontiguousarray(mask[order].transpose(2, 0, 1).reshape(CHUNK, H * CHUNK))
    lamq_t = np.zeros((128, KC * CHUNK), np.float32)
    lamk_t = np.zeros((128, KC * CHUNK), np.float32)
    for m in range(KC):
        for p_ in range(128):
            lamq_t[p_, m * CHUNK:(m + 1) * CHUNK] = lam_q[2 * m + p_ // 64]
        for f in range(128):
            lamk_t[:, m * CHUNK + f] = lam_k[2 * m + f // 64]
    lam_c = np.exp(-s * CHUNK).astype(np.float32)
    lamc_t = np.zeros((64, H * D), np.float32)
    for h in range(H):
        lamc_t[:, h * D:(h + 1) * D] = lam_c[h]
    seg, bat = core % 4, core // 4
    pw = np.zeros((8, 64, H * D), np.float32)
    lam_seg = np.exp(-s * np.float32(T)).astype(np.float32)
    for cp in range(8):
        if cp // 4 != bat or cp % 4 >= seg:
            continue
        wv = (lam_seg ** np.float32(seg - 1 - (cp % 4))).astype(np.float32)
        for h in range(H):
            pw[cp, :, h * D:(h + 1) * D] = wv[h]
    j2 = (np.eye(128)[0:64] + np.eye(128, k=64)[0:64]).astype(ml_dtypes.bfloat16)
    return dict(maskT=maskT, lamq=lamq_t, lamk=lamk_t, lamc=lamc_t,
                prefw=pw.reshape(8 * 64, H * D),
                ident=np.eye(128, dtype=ml_dtypes.bfloat16),
                ones=np.ones((1, 128), np.float32),
                j2=j2, ones_col=np.ones((128, 1), np.float32))


class K:

    def __init__(self, stage, debug):
        self.stage = stage
        self.debug = debug
        self.nc = bass.Bass("TRN2", target_bir_lowering=False)
        self.streams = {e: [] for e in ENGINES}
        self.counts = {}
        self.sb = {}
        self.ps = {}
        self.sems = {}
        self.dram = {}
        self.marks = {}

    def on(self, eng, fn):
        self.streams[eng].append(fn)

    def bump(self, sem, by=1):
        v = self.counts.get(sem, 0) + by
        self.counts[sem] = v
        return v

    def cur(self, sem):
        return self.counts.get(sem, 0)

    def din(self, name, shape, dtype=F32):
        ap = self.nc.dram_tensor(name, list(shape), dtype, kind="ExternalInput").ap()
        self.dram[name] = ap
        return ap

    def dout(self, name, shape, dtype=F32):
        ap = self.nc.dram_tensor(name, list(shape), dtype, kind="ExternalOutput").ap()
        self.dram[name] = ap
        return ap

    def dint(self, name, shape, dtype=F32):
        ap = self.nc.dram_tensor(name, list(shape), dtype).ap()
        self.dram[name] = ap
        return ap

    def S(self, name, shape, dtype):
        t = self.cur_es.enter_context(self.nc.sbuf_tensor(name, list(shape), dtype))
        self.sb[name] = t
        return t

    def declare_f1(self):
        S = self.S
        S("wgT", (128, KC * I), BF16)
        S("wuT", (128, KC * I), BF16)
        S("lsc", (128, E // 32), F32)
        S("lsci", (128, E // 32), F32)
        S("lse", (128, E // 2), F32)
        S("lse2", (128, E // 2), BF16)
        S("lwy", (128, E // 2), F32)
        S("lwr", (128, E), BF16)
        S("lqn", (128, E), BF16)
        S("xq", (128, 2 * KC * 512), BF16)
        S("f1e", (128, 2 * 512), BF16)

    def P(self, name, shape, dtype=F32):
        t = self.cur_es.enter_context(self.nc.psum_tensor(name, list(shape), dtype))
        self.ps[name] = t
        return t

    def SEM(self, name):
        s = self.cur_es.enter_context(self.nc.semaphore(name))
        self.sems[name] = s
        return s

    def declare(self):
        d = self.din
        d("x", (T, E)); d("wq", (E, E)); d("wk", (E, E)); d("wv", (E, E))
        d("wo", (E, E)); d("wg1", (E, D)); d("wg2", (D, E))
        d("gamma_fm", (128, KC)); d("beta_fm", (128, KC))
        d("gamma_row", (1, E)); d("beta_row", (1, E))
        d("w_up", (I, E)); d("w_gate", (I, E)); d("w_down", (E, I))
        d("maskT", (128, H * 128)); d("lamq", (128, KC * 128))
        d("lamk", (128, KC * 128)); d("lamc", (64, H * D))
        d("prefw", (8 * 64, H * D)); d("ident", (128, 128), BF16)
        d("ones", (1, 128)); d("j2", (64, 128), BF16); d("ones_col", (128, 1))
        self.dout("out", (T, E))
        if self.debug:
            self.dout("d_xnfm", (128, KC * T), BF16)
            self.dout("d_sca", (128, 32))
            self.dout("d_st", (128, 6 * C))
            self.dout("d_graw", (128, 4 * 512), BF16)
            self.dout("d_uraw", (128, 4 * 512), BF16)
            self.dout("d_wgq", (128, I), BF16)
            self.dout("d_S", (64, H * D))
            self.dout("d_kvin", (64, H * D))
            self.dout("d_q", (128, E), BF16)
            self.dout("d_k", (128, E), BF16)
            self.dout("d_v", (128, H * D), BF16)
            self.dout("d_o", (128, E), BF16)
            self.dout("d_oln", (128, KC * 128))
            self.dout("d_sca2", (128, 24))
            self.dout("d_sca3", (128, 24))
            self.dout("d_sca4", (128, 64))
            self.dout("d_sca5", (128, 64))
            self.dout("d_t", (128, 512), BF16)
            self.dout("d_u2", (128, 512), BF16)

        self.dint("spill_g", (I, T), BF16)
        self.dint("spill_u", (I, T), BF16)
        self.dint("spill_t", (I, T), BF16)
        self.dint("spill_w", (I, T), BF16)
        self.dint("spill_k", (128, KC * T), BF16)
        self.dint("spill_v", (128, C * H * D), BF16)
        self.dint("spill_wd", (128, IC * 1024), BF16)
        self.dint("cc_ar1i", (1, 128)); self.dint("cc_ar1o", (1, 128))
        self.dint("cc_ar2i", (1, 128)); self.dint("cc_ar2o", (1, 128))
        self.dint("cc_ar3i", (1, 128)); self.dint("cc_ar3o", (1, 128))
        self.dint("cc_ar4i", (1, 128)); self.dint("cc_ar4o", (1, 128))
        self.dint("cc_ar5i", (1, 128)); self.dint("cc_ar5o", (1, 128))
        self.dint("cc_agi", (64, H * D)); self.dint("cc_ago", (8 * 64, H * D))
        self.dint("scr_d", (8, 128))

        S, P, SEM = self.S, self.P, self.SEM
        S("xn_fm", (128, KC * T), BF16)
        S("maskTs", (128, H * 128), F32)
        S("lamqs", (128, KC * 128), F32)
        S("lamks", (128, KC * 128), F32)
        S("lamcs", (64, H * D), F32)
        S("idents", (128, 128), BF16)
        S("oness", (1, 128), F32)
        S("j2s", (64, 128), BF16)
        S("onescol", (128, 1), F32)
        S("gam", (128, KC), F32)
        S("bet", (128, KC), F32)
        S("gam_row", (1, E), F32)
        S("bet_row", (1, E), F32)
        S("sca", (128, 32), F32)
        S("kcn", (128, 16), F32)
        for fam in ("q_sx", "q_sg", "q_su", "q_st", "q_so"):
            S(fam, (128, 64), F32)
        S("row", (1, 256), F32)
        S("xld", (128, 2 * E), F32)
        S("sqs", (128, E), F32)
        S("st_sums", (128, C), F32)
        S("st_sqs", (128, C), F32)
        S("st_m", (128, C), F32)
        S("st_r", (128, C), F32)
        S("st_nmr", (128, C), F32)
        S("st_max", (128, C), F32)
        S("st_a", (128, C), F32)
        S("st_b", (128, C), F32)
        S("xn_tm", (128, 2 * E), BF16)
        S("st_gmax", (128, 192), F32)
        for b_ in range(8):
            P("pb%d" % b_, (128, 512), F32)

        for s_ in ("s_x", "s_w", "s_pe", "s_dve", "s_act", "s_pool", "s_out",
                   "s_sp", "s_sl", "s_st",
                   "c_ar1", "c_ar2", "c_ar3", "c_ar4", "c_ar5", "c_ag"):
            SEM(s_)

    SXMAXC, SXC, ISXC, SGC, ISGC, SUC, ISUC, STC, ISTC, SOC, ISOC, TMP0, TMP1 = range(13)

    def p0(self):
        sb, ps, sems = self.sb, self.ps, self.sems
        x_h, out_h = self.dram["x"], self.dram["out"]

        def c_loads2(sync):
            for nm, src in (("maskTs", "maskT"), ("lamqs", "lamq"), ("lamks", "lamk"),
                            ("lamcs", "lamc"), ("oness", "ones"), ("gam", "gamma_fm"),
                            ("bet", "beta_fm"), ("j2s", "j2"), ("onescol", "ones_col"),
                            ("gam_row", "gamma_row"), ("bet_row", "beta_row"),
                            ("idents", "ident")):
                sync.dma_start(out=sb[nm][:], in_=self.dram[src]).then_inc(sems["s_x"], 16)
        self.on("sync", c_loads2)
        consts_v = self.bump("s_x", 16 * 12)
        self.marks["consts"] = consts_v

        def z0(vector):
            vector.memset(sb["row"][:], 1.0)
            vector.memset(sb["kcn"][:], 0.0)
            vector.memset(sb["kcn"][:, 0:1], EPS)
            for fam in ("q_sx", "q_sg", "q_su", "q_st", "q_so"):
                vector.memset(sb[fam][:], 1.0)
            vector.memset(sb["sca"][:], 1.0).then_inc(sems["s_dve"], 1)
        self.on("vector", z0)
        self.bump("s_dve", 1)

        sq_done = {}
        rs_done = {}
        for ci in range(C):
            buf = ci % 2
            sl = (buf * E, (buf + 1) * E)
            waits = []
            if ci >= 2:
                waits = [("s_act", sq_done[ci - 2]), ("s_dve", rs_done[ci - 2]),
                         ("s_out", 16 * (ci - 1))]
            def ld(sync, ci=ci, sl=sl, waits=tuple(waits)):
                for s_, v_ in waits:
                    sync.wait_ge(sems[s_], v_)
                sync.dma_start(out=sb["xld"][:, sl[0]:sl[1]],
                               in_=x_h[ci * 128:(ci + 1) * 128, :]).then_inc(sems["s_x"], 16)
            self.on("sync", ld)
            xv = self.bump("s_x", 16)

            def stx(sync, ci=ci, sl=sl, xv=xv):
                sync.wait_ge(sems["s_x"], xv)
                sync.dma_start(out=out_h[ci * 128:(ci + 1) * 128, :],
                               in_=sb["xld"][:, sl[0]:sl[1]]).then_inc(sems["s_out"], 16)
            self.on("sync", stx)
            self.bump("s_out", 16)

            def sq(scalar, ci=ci, sl=sl, xv=xv):
                scalar.wait_ge(sems["s_x"], xv)
                scalar.activation(sb["sqs"][:], sb["xld"][:, sl[0]:sl[1]], AF.Square,
                                  accum_out=sb["st_sqs"][:, ci:ci + 1]).then_inc(sems["s_act"], 1)
            self.on("scalar", sq)
            sq_done[ci] = self.bump("s_act", 1)

            def rs(vector, ci=ci, sl=sl, xv=xv):
                vector.wait_ge(sems["s_x"], xv)
                vector.reduce_sum(sb["st_sums"][:, ci:ci + 1], sb["xld"][:, sl[0]:sl[1]],
                                  AX).then_inc(sems["s_dve"], 1)
            self.on("vector", rs)
            rs_done[ci] = self.bump("s_dve", 1)
        self.marks["x_stored"] = self.cur("s_out")

        all_sq = sq_done[C - 1]
        all_rs = rs_done[C - 1]
        def bstat(vector, all_sq=all_sq, all_rs=all_rs, base=self.cur("s_dve")):
            v = vector
            v.wait_ge(sems["s_act"], all_sq)
            v.wait_ge(sems["s_dve"], all_rs)
            v.tensor_scalar(sb["st_m"][:], sb["st_sums"][:], 1.0 / E, None, ALU.mult).then_inc(sems["s_dve"], 1)
            v.tensor_scalar(sb["st_a"][:], sb["st_sqs"][:], 1.0 / E, None, ALU.mult).then_inc(sems["s_dve"], 1)
            v.wait_ge(sems["s_dve"], base + 2)
            v.tensor_tensor(sb["st_b"][:], sb["st_m"][:], sb["st_m"][:], ALU.mult).then_inc(sems["s_dve"], 1)
            v.wait_ge(sems["s_dve"], base + 3)
            v.tensor_tensor(sb["st_a"][:], sb["st_a"][:], sb["st_b"][:], ALU.subtract).then_inc(sems["s_dve"], 1)
        self.on("vector", bstat)
        self.bump("s_dve", 3)
        bv = self.bump("s_dve", 1)

        def bsqrt(scalar, bv=bv):
            scalar.wait_ge(sems["s_dve"], bv)
            scalar.activation(sb["st_b"][:], sb["st_a"][:], AF.Sqrt,
                              bias=sb["kcn"][:, 0:1]).then_inc(sems["s_act"], 1)
        self.on("scalar", bsqrt)
        bs = self.bump("s_act", 1)

        def brec(vector, bs=bs, base=self.cur("s_dve")):
            v = vector
            v.wait_ge(sems["s_act"], bs)
            v.reciprocal(sb["st_r"][:], sb["st_b"][:]).then_inc(sems["s_dve"], 1)
            v.wait_ge(sems["s_dve"], base + 1)
            v.tensor_tensor(sb["st_a"][:], sb["st_m"][:], sb["st_r"][:], ALU.mult).then_inc(sems["s_dve"], 1)
            v.wait_ge(sems["s_dve"], base + 2)
            v.tensor_scalar(sb["st_nmr"][:], sb["st_a"][:], -1.0, None, ALU.mult).then_inc(sems["s_dve"], 1)
        self.on("vector", brec)
        self.bump("s_dve", 2)
        stats_done = self.bump("s_dve", 1)
        self.marks["ln_stats"] = stats_done

        apply_act = {}
        evac_act = {}
        mx_done = {}
        for ci in range(C):
            buf = ci % 2
            sl = (buf * E, (buf + 1) * E)
            waits = [("s_dve", stats_done), ("s_out", self.marks["x_stored"])]
            if ci >= 2:
                waits.append(("s_act", apply_act[ci - 2]))
            def ld(sync, ci=ci, sl=sl, waits=tuple(waits)):
                for s_, v_ in waits:
                    sync.wait_ge(sems[s_], v_)
                sync.dma_start(out=sb["xld"][:, sl[0]:sl[1]],
                               in_=x_h[ci * 128:(ci + 1) * 128, :]).then_inc(sems["s_x"], 16)
            self.on("sync", ld)
            xv = self.bump("s_x", 16)

            tw = []
            if ci >= 2:
                tw = [("s_pe", self.marks[("p0tr", ci - 2)]), ("s_dve", mx_done[ci - 2])]
            def ap(scalar, ci=ci, sl=sl, xv=xv, tw=tuple(tw)):
                scalar.wait_ge(sems["s_x"], xv)
                for s_, v_ in tw:
                    scalar.wait_ge(sems[s_], v_)
                scalar.activation(sb["xn_tm"][:, sl[0]:sl[1]], sb["xld"][:, sl[0]:sl[1]],
                                  AF.Identity, bias=sb["st_nmr"][:, ci:ci + 1],
                                  scale=sb["st_r"][:, ci:ci + 1]).then_inc(sems["s_act"], 1)
            self.on("scalar", ap)
            apply_act[ci] = self.bump("s_act", 1)

            def mx(vector, ci=ci, sl=sl, ad=apply_act[ci]):
                vector.wait_ge(sems["s_act"], ad)
                vector.reduce_max(sb["st_max"][:, ci:ci + 1], sb["xn_tm"][:, sl[0]:sl[1]],
                                  AX, apply_absolute_value=True).then_inc(sems["s_dve"], 1)
            self.on("vector", mx)
            mx_done[ci] = self.bump("s_dve", 1)

            pw = []
            if ci >= 2:
                pw = [("s_act", evac_act[ci - 2])]
            def tr(tensor, ci=ci, sl=sl, buf=buf, ad=apply_act[ci], pw=tuple(pw),
                   cv=consts_v):
                tensor.wait_ge(sems["s_x"], cv)
                tensor.wait_ge(sems["s_act"], ad)
                for s_, v_ in pw:
                    tensor.wait_ge(sems[s_], v_)
                pt = ps["pb%d" % buf][:].bitcast(BF16)
                for m in range(KC):
                    ins = tensor.transpose(
                        pt[:, m * 128:(m + 1) * 128],
                        sb["xn_tm"][:, sl[0] + m * 128: sl[0] + (m + 1) * 128],
                        sb["idents"][:])
                    if m == KC - 1:
                        ins.then_inc(sems["s_pe"], 1)
            self.on("tensor", tr)
            self.marks[("p0tr", ci)] = self.bump("s_pe", 1)

            def ev(scalar, ci=ci, buf=buf, pv=self.marks[("p0tr", ci)]):
                scalar.wait_ge(sems["s_pe"], pv)
                pt = ps["pb%d" % buf][:].bitcast(BF16)
                for m in range(KC):
                    ins = scalar.copy(
                        sb["xn_fm"][:, m * T + ci * 128:m * T + (ci + 1) * 128],
                        pt[:, m * 128:(m + 1) * 128])
                    if m == KC - 1:
                        ins.then_inc(sems["s_act"], 1)
            self.on("scalar", ev)
            evac_act[ci] = self.bump("s_act", 1)

        self.marks["xn_done"] = evac_act[C - 1]

        pool_all = mx_done[C - 1]
        def gmax(vector, pool_all=pool_all):
            vector.wait_ge(sems["s_dve"], pool_all)
            vector.reduce_max(sb["sca"][:, 0:1], sb["st_max"][:], AX).then_inc(sems["s_dve"], 1)
        self.on("vector", gmax)
        gm = self.bump("s_dve", 1)

        self.ar_chain("ar1", sb["sca"][:, 0:1], [sb["q_sx"][:, 0:1]], ("s_dve", gm))
        self.scale_chain("q_sx", QMAX_A, self.marks["ar1_done"])
        self.marks["sx_ready"] = self.marks["q_sx_ready"]

    def ar_chain(self, tag, src_col, dst_col, wait):
        p, sb, sems = self, self.sb, self.sems
        cci = self.dram["cc_" + tag + "i"]
        cco = self.dram["cc_" + tag + "o"]
        scr = self.dram["scr_d"]
        roff = {"ar1": 0, "ar2": 1, "ar3": 2, "ar4": 3, "ar5": 4}[tag]

        nred_in = cci.shape[1] // 128
        def sp(sync, wait=wait, nred_in=nred_in):
            sync.wait_ge(sems[wait[0]], wait[1])
            dst = cci.rearrange("u (p j) -> p (u j)", j=nred_in)
            sync.dma_start(out=dst, in_=src_col).then_inc(sems["s_st"], 16)
        self.on("sync", sp)
        spv = self.bump("s_st", 16)

        def ar(gpsimd, spv=spv):
            gpsimd.wait_ge(sems["s_st"], spv)
            gpsimd.collective_compute("AllReduce", ALU.max,
                                      replica_groups=[list(range(N_CORES))],
                                      ins=[cci], outs=[cco]).then_inc(sems["c_" + tag], 1)
        self.on("gpsimd", ar)
        self.bump("c_" + tag, 1)

        def ld(sync):
            sync.wait_ge(sems["c_" + tag], 1)
            sync.dma_start(out=sb["row"][0:1, 0:cco.shape[1]], in_=cco).then_inc(sems["s_st"], 16)
        self.on("sync", ld)
        ldv = self.bump("s_st", 16)

        nred = cco.shape[1] // 128
        def red(vector, ldv=ldv, nred=nred):
            vector.wait_ge(sems["s_st"], ldv)
            grp = sb["row"][0:1, 0:128 * nred].rearrange("o (p j) -> o p j", j=nred)
            for j in range(nred):
                vector.reduce_max(sb["row"][0:1, 200 + j:201 + j], grp[:, :, j], AX)
            vector.engine_nop().then_inc(sems["s_dve"], 1)
        self.on("vector", red)
        redv = self.bump("s_dve", 1)

        def sp2(sync, redv=redv, nred=nred, roff=roff):
            sync.wait_ge(sems["s_dve"], redv)
            sync.dma_start(out=scr[roff:roff + 1, 0:nred],
                           in_=sb["row"][0:1, 200:200 + nred]).then_inc(sems["s_st"], 16)
        self.on("sync", sp2)
        sp2v = self.bump("s_st", 16)

        dsts = dst_col if isinstance(dst_col, (list, tuple)) else [dst_col]
        def bc(sync, sp2v=sp2v, roff=roff):
            sync.wait_ge(sems["s_st"], sp2v)
            for j, dc in enumerate(dsts):
                sync.dma_start(out=dc,
                               in_=scr[roff:roff + 1, j:j + 1].broadcast_to((128, 1))).then_inc(sems["s_st"], 16)
        self.on("sync", bc)
        self.marks[tag + "_done"] = self.bump("s_st", 16 * len(dsts))

    def scale_chain(self, tile, qmax, wait_mark):
        sems, sb = self.sems, self.sb
        t = sb[tile]
        def f(vector, wait_mark=wait_mark, base=self.cur("s_dve")):
            v = vector
            v.wait_ge(sems["s_st"], wait_mark)
            v.tensor_copy(t[:, 0:16], t[:, 0:1].broadcast_to((128, 16))).then_inc(sems["s_dve"], 1)
            v.wait_ge(sems["s_dve"], base + 1)
            v.tensor_scalar(t[:, 16:32], t[:, 0:16], 1.0 / qmax, 1e-12, ALU.mult, ALU.max).then_inc(sems["s_dve"], 1)
            v.wait_ge(sems["s_dve"], base + 2)
            v.reciprocal(t[:, 32:48], t[:, 16:32]).then_inc(sems["s_dve"], 1)
            v.tensor_scalar(t[:, 48:64], t[:, 16:32], -MAGIC, None,
                            ALU.mult).then_inc(sems["s_dve"], 1)
        self.on("vector", f)
        self.bump("s_dve", 3)
        self.marks[tile + "_ready"] = self.bump("s_dve", 1)

    def tail(self):
        p, sems = self, self.sems
        total = self.cur("s_out")
        def fin(gpsimd, total=total):
            gpsimd.wait_ge(sems["s_out"], total)
        self.on("gpsimd", fin)

        if self.debug:
            sb = self.sb
            marks = dict(self.marks)
            def dmp(sync):
                if "xn_done" in marks:
                    sync.wait_ge(sems["s_act"], marks["xn_done"])
                if "sx_ready" in marks:
                    sync.wait_ge(sems["s_dve"], marks["sx_ready"])
                sync.dma_start(out=self.dram["d_xnfm"], in_=sb["xn_fm"][:]).then_inc(sems["s_out"], 16)
                sync.dma_start(out=self.dram["d_sca"], in_=sb["q_sx"][:, 0:32]).then_inc(sems["s_out"], 16)
                for j, nm in enumerate(("st_sums", "st_sqs", "st_m", "st_r", "st_nmr", "st_max")):
                    sync.dma_start(out=self.dram["d_st"][:, j * C:(j + 1) * C],
                                   in_=sb[nm][:]).then_inc(sems["s_out"], 16)
            self.on("sync", dmp)
            self.bump("s_out", 32 + 6 * 16)
            total2 = self.cur("s_out")
            def fin2(gpsimd, total2=total2):
                gpsimd.wait_ge(sems["s_out"], total2)
            self.on("gpsimd", fin2)

    def build(self):
        self.es = ExitStack()
        with self.es:
            self.cur_es = self.es
            self.declare()
            with ExitStack() as es1:
                self.cur_es = es1
                self.declare_f1()
                self.p0()
                if self.stage >= 1:
                    self.p1()
                self.cur_es = self.es
            if self.stage >= 2:
                with ExitStack() as es2:
                    self.cur_es = es2
                    self.p2()
                    self.cur_es = self.es
            if self.stage >= 3:
                with ExitStack() as es3:
                    self.cur_es = es3
                    self.p3()
                    self.cur_es = self.es
            self.tail()
            block = self.es.enter_context(self.nc.Block())
            for eng_name in ENGINES:
                deco = getattr(block, eng_name)
                stream = self.streams[eng_name]
                def make(stream):
                    def body(eng):
                        for fn in stream:
                            fn(eng)
                    return body
                deco(make(stream))
        return self.nc

    def p1(self):
        self.p1_weights("w_gate", "wgT")
        self.p1_xq_and_mm("w_gate")
        self.p1_weights("w_up", "wuT")
        self.p1_xq_and_mm("w_up")
        self.p1_wd()
        if self.debug:
            self.p1_debug()

    def p1_weights(self, wname, dst):
        sb, ps, sems = self.sb, self.ps, self.sems
        w_h = self.dram[wname]
        tr_ev = {}
        for c in range(IC):
            buf = c % 2
            sl = (buf * E, (buf + 1) * E)
            waits = []
            if c >= 2:
                waits.append(("s_dve", self.marks[(wname, "dq", c - 2)]))
            else:
                waits.append(("s_act", self.marks["xn_done"]))
            def ld(sync, c=c, sl=sl, waits=tuple(waits)):
                for s_, v_ in waits:
                    sync.wait_ge(sems[s_], v_)
                sync.dma_start(out=sb["xld"][:, sl[0]:sl[1]],
                               in_=w_h[c * 128:(c + 1) * 128, :]).then_inc(sems["s_w"], 16)
            self.on("sync", ld)
            wv = self.bump("s_w", 16)

            twait = []
            if c >= 1:
                twait.append(("s_pe", self.marks[(wname, "tr", c - 1)]))
            def dq(vector, c=c, sl=sl, wv=wv, base=self.cur("s_dve"), twait=tuple(twait)):
                v = vector
                v.wait_ge(sems["s_w"], wv)
                for s_, v_ in twait:
                    v.wait_ge(sems[s_], v_)
                lw = sb["xld"][:, sl[0]:sl[1]]
                v.tensor_reduce(sb["lsc"][:].rearrange("p (b u) -> p b u", u=1),
                                lw.rearrange("p (b u) -> p b u", u=32),
                                AX, ALU.max, apply_absolute_value=True).then_inc(sems["s_dve"], 1)
                v.wait_ge(sems["s_dve"], base + 1)
                v.tensor_scalar(sb["lsc"][:], sb["lsc"][:], 1.0 / QMAX_W, 1e-12,
                                ALU.mult, ALU.max).then_inc(sems["s_dve"], 1)
                v.wait_ge(sems["s_dve"], base + 2)
                v.reciprocal(sb["lsci"][:], sb["lsc"][:]).then_inc(sems["s_dve"], 1)
                n = 3
                for hf in range(2):
                    hs = slice(hf * 512, (hf + 1) * 512)
                    bs = slice(hf * 16, (hf + 1) * 16)
                    v.wait_ge(sems["s_dve"], base + n)
                    v.tensor_copy(sb["lse"][:].rearrange("p (b u) -> p b u", u=32),
                                  sb["lsci"][:, bs].rearrange("p (b u) -> p b u", u=1).broadcast_to((128, 16, 32)))
                    v.tensor_copy(sb["lse2"][:].rearrange("p (b u) -> p b u", u=32),
                                  sb["lsc"][:, bs].rearrange("p (b u) -> p b u", u=1).broadcast_to((128, 16, 32))).then_inc(sems["s_dve"], 1)
                    n += 1
                    v.wait_ge(sems["s_dve"], base + n)
                    v.tensor_tensor(sb["lwy"][:], lw[:, hs], sb["lse"][:], ALU.mult).then_inc(sems["s_dve"], 1)
                    n += 1
                    v.wait_ge(sems["s_dve"], base + n)
                    v.tensor_scalar(sb["lwr"][:, hs], sb["lwy"][:], MAGIC, MAGIC,
                                    ALU.add, ALU.subtract).then_inc(sems["s_dve"], 1)
                    n += 1
                    v.wait_ge(sems["s_dve"], base + n)
                    v.tensor_tensor(sb["lqn"][:, hs], sb["lwr"][:, hs], sb["lse2"][:],
                                    ALU.mult).then_inc(sems["s_dve"], 1)
                    n += 1
            self.on("vector", dq)
            self.bump("s_dve", 10)
            self.marks[(wname, "dq", c)] = self.bump("s_dve", 1)

            pw = []
            if c >= 2:
                pw.append(("s_act", tr_ev[c - 2]))
            def tr(tensor, c=c, buf=buf, dv=self.marks[(wname, "dq", c)], pw=tuple(pw),
                   cv=self.marks["consts"]):
                tensor.wait_ge(sems["s_x"], cv)
                tensor.wait_ge(sems["s_dve"], dv)
                for s_, v_ in pw:
                    tensor.wait_ge(sems[s_], v_)
                pt = ps["pb%d" % (4 + buf)][:].bitcast(BF16)
                for kk in range(KC):
                    ins = tensor.transpose(
                        pt[:, kk * 128:(kk + 1) * 128],
                        sb["lqn"][:, kk * 128:(kk + 1) * 128],
                        sb["idents"][:])
                    if kk == KC - 1:
                        ins.then_inc(sems["s_pe"], 1)
            self.on("tensor", tr)
            self.marks[(wname, "tr", c)] = self.bump("s_pe", 1)

            def ev(scalar, c=c, buf=buf, dst=dst, pv=self.marks[(wname, "tr", c)]):
                scalar.wait_ge(sems["s_pe"], pv)
                pt = ps["pb%d" % (4 + buf)][:].bitcast(BF16)
                for kk in range(KC):
                    ins = scalar.copy(
                        sb[dst][:, kk * I + c * 128:kk * I + (c + 1) * 128],
                        pt[:, kk * 128:(kk + 1) * 128])
                    if kk == KC - 1:
                        ins.then_inc(sems["s_act"], 1)
            self.on("scalar", ev)
            tr_ev[c] = self.bump("s_act", 1)
            self.marks[(wname, "ev", c)] = tr_ev[c]
        self.marks[(wname, "ready")] = tr_ev[IC - 1]

    def p1_wd(self):
        sb, ps, sems = self.sb, self.ps, self.sems
        w_h = self.dram["w_down"]
        ev_m = {}
        for sc in range(24):
            c, t = sc // 3, sc % 3
            buf = sc % 2
            sl = (buf * E, (buf + 1) * E)
            waits = []
            if sc >= 2:
                waits.append(("s_dve", self.marks[("wd", "dq", sc - 2)]))
            else:
                waits.append(("s_dve", self.marks[("w_up", "dq", IC - 2 + sc)]))
            def ld(sync, c=c, t=t, sl=sl, waits=tuple(waits)):
                for s_, v_ in waits:
                    sync.wait_ge(sems[s_], v_)
                sync.dma_start(out=sb["xld"][:, sl[0]:sl[1]],
                               in_=w_h[c * 128:(c + 1) * 128,
                                       t * 1024:(t + 1) * 1024]).then_inc(sems["s_w"], 16)
            self.on("sync", ld)
            wv = self.bump("s_w", 16)

            twait = []
            if sc >= 1:
                twait.append(("s_pe", self.marks[("wd", "tr", sc - 1)]))
            else:
                twait.append(("s_pe", self.marks[("w_up", "tr", IC - 1)]))
            def dq(vector, sl=sl, wv=wv, base=self.cur("s_dve"), twait=tuple(twait)):
                v = vector
                v.wait_ge(sems["s_w"], wv)
                for s_, v_ in twait:
                    v.wait_ge(sems[s_], v_)
                lw = sb["xld"][:, sl[0]:sl[1]]
                v.tensor_reduce(sb["lsc"][:].rearrange("p (b u) -> p b u", u=1),
                                lw.rearrange("p (b u) -> p b u", u=32),
                                AX, ALU.max, apply_absolute_value=True).then_inc(sems["s_dve"], 1)
                v.wait_ge(sems["s_dve"], base + 1)
                v.tensor_scalar(sb["lsc"][:], sb["lsc"][:], 1.0 / QMAX_W, 1e-12,
                                ALU.mult, ALU.max).then_inc(sems["s_dve"], 1)
                v.wait_ge(sems["s_dve"], base + 2)
                v.reciprocal(sb["lsci"][:], sb["lsc"][:]).then_inc(sems["s_dve"], 1)
                n = 3
                for hf in range(2):
                    hs = slice(hf * 512, (hf + 1) * 512)
                    bs = slice(hf * 16, (hf + 1) * 16)
                    v.wait_ge(sems["s_dve"], base + n)
                    v.tensor_copy(sb["lse"][:].rearrange("p (b u) -> p b u", u=32),
                                  sb["lsci"][:, bs].rearrange("p (b u) -> p b u", u=1).broadcast_to((128, 16, 32)))
                    v.tensor_copy(sb["lse2"][:].rearrange("p (b u) -> p b u", u=32),
                                  sb["lsc"][:, bs].rearrange("p (b u) -> p b u", u=1).broadcast_to((128, 16, 32))).then_inc(sems["s_dve"], 1)
                    n += 1
                    v.wait_ge(sems["s_dve"], base + n)
                    v.tensor_tensor(sb["lwy"][:], lw[:, hs], sb["lse"][:], ALU.mult).then_inc(sems["s_dve"], 1)
                    n += 1
                    v.wait_ge(sems["s_dve"], base + n)
                    v.tensor_scalar(sb["lwr"][:, hs], sb["lwy"][:], MAGIC, MAGIC,
                                    ALU.add, ALU.subtract).then_inc(sems["s_dve"], 1)
                    n += 1
                    v.wait_ge(sems["s_dve"], base + n)
                    v.tensor_tensor(sb["lqn"][:, hs], sb["lwr"][:, hs], sb["lse2"][:],
                                    ALU.mult).then_inc(sems["s_dve"], 1)
                    n += 1
            self.on("vector", dq)
            self.bump("s_dve", 10)
            self.marks[("wd", "dq", sc)] = self.bump("s_dve", 1)

            pw = []
            if sc >= 2:
                pw.append(("s_act", ev_m[sc - 2]))
            else:
                pw.append(("s_act", self.marks[("w_up", "ev", IC - 2 + sc)]))
            def tr(tensor, buf=buf, dv=self.marks[("wd", "dq", sc)], pw=tuple(pw)):
                tensor.wait_ge(sems["s_dve"], dv)
                for s_, v_ in pw:
                    tensor.wait_ge(sems[s_], v_)
                pt = ps["pb%d" % (4 + buf)][:].bitcast(BF16)
                for kk in range(KC):
                    ins = tensor.transpose(
                        pt[:, kk * 128:(kk + 1) * 128],
                        sb["lqn"][:, kk * 128:(kk + 1) * 128],
                        sb["idents"][:])
                    if kk == KC - 1:
                        ins.then_inc(sems["s_pe"], 1)
            self.on("tensor", tr)
            self.marks[("wd", "tr", sc)] = self.bump("s_pe", 1)

            spw = self.marks.get(("wd", "sp", sc - 1))
            if spw is None:
                spw = self.marks[("w_up", "sp", 3, IC - 1)]
            def ev(scalar, buf=buf, pv=self.marks[("wd", "tr", sc)], spw=spw):
                scalar.wait_ge(sems["s_pe"], pv)
                if spw is not None:
                    scalar.wait_ge(sems["s_sp"], spw)
                pt = ps["pb%d" % (4 + buf)][:].bitcast(BF16)
                scalar.copy(sb["f1e"][:], pt[:]).then_inc(sems["s_act"], 1)
            self.on("scalar", ev)
            ev_m[sc] = self.bump("s_act", 1)

            def sp(sync, c=c, t=t, evv=ev_m[sc]):
                sync.wait_ge(sems["s_act"], evv)
                dst = self.dram["spill_wd"].rearrange("p (k e) -> p k e", e=1024)[
                    :, t * 8:(t + 1) * 8, c * 128:(c + 1) * 128]
                sync.dma_start(out=dst,
                               in_=sb["f1e"][:].rearrange("p (k e) -> p k e", e=128)).then_inc(sems["s_sp"], 16)
            self.on("sync", sp)
            self.marks[("wd", "sp", sc)] = self.bump("s_sp", 16)
        self.marks["wd_ready"] = self.marks[("wd", "sp", 23)]

    def p1_xq_and_mm(self, wname):
        sb, ps, sems = self.sb, self.ps, self.sems
        is_gate = wname == "w_gate"
        wt = "wgT" if is_gate else "wuT"
        spill = self.dram["spill_g" if is_gate else "spill_u"]
        gx0 = 0 if is_gate else 96
        mm_ev = {}
        for g in range(4):
            buf = g % 2
            xsl = (buf * KC * 512, (buf + 1) * KC * 512)
            waits = [("s_dve", self.marks["q_sx_ready"]), ("s_act", self.marks["xn_done"])]
            if g >= 2:
                waits.append(("s_pe", self.marks[(wname, "mm", g - 2, IC - 1)]))
            elif not is_gate:
                waits.append(("s_pe", self.marks[("w_gate", "mm", g + 2, IC - 1)]))
            def xq(vector, g=g, xsl=xsl, waits=tuple(waits), base=self.cur("s_dve")):
                v = vector
                for s_, v_ in waits:
                    v.wait_ge(sems[s_], v_)
                nb = 0
                for sub in range(4):
                    kk0 = sub * 2
                    src = sb["xn_fm"][:].rearrange("p (k t) -> p k t", t=T)[
                        :, kk0:kk0 + 2, g * 512:(g + 1) * 512]
                    v.tensor_scalar(sb["sqs"][:].rearrange("p (k t) -> p k t", t=512),
                                    src, sb["q_sx"][:, 32:33], MAGIC,
                                    ALU.mult, ALU.add).then_inc(sems["s_dve"], 1)
                    nb += 1
                    v.wait_ge(sems["s_dve"], base + nb)
                    dstq = sb["xq"][:, xsl[0] + kk0 * 512: xsl[0] + (kk0 + 2) * 512]
                    v.tensor_scalar(dstq, sb["sqs"][:], -MAGIC, sb["q_sx"][:, 16:17],
                                    ALU.add, ALU.mult).then_inc(sems["s_dve"], 1)
                    nb += 1
                    v.wait_ge(sems["s_dve"], base + nb)
                v.engine_nop().then_inc(sems["s_dve"], 1)
            self.on("vector", xq)
            self.bump("s_dve", 8)
            xq_done = self.bump("s_dve", 1)
            self.marks[(wname, "xq", g)] = xq_done

            for m in range(IC):
                bank = "pb2" if m % 2 == 0 else "pb3"
                mw = [("s_dve", xq_done), ("s_act", self.marks[(wname, "ev", m)])]
                if (g, m) != (0, 0) and (g, m) != (0, 1):
                    pm, pg = (m - 2, g) if m >= 2 else (IC - 2 + m, g - 1)
                    mw.append(("s_act", mm_ev[(pg, pm)]))
                elif not is_gate:
                    mw.append(("s_act", self.marks[("w_gate", "ev2", 3, IC - 2 + m)]))
                def mm(tensor, g=g, m=m, bank=bank, xsl=xsl, wt=wt, mw=tuple(mw)):
                    for s_, v_ in mw:
                        tensor.wait_ge(sems[s_], v_)
                    for k in range(KC):
                        ins = tensor.matmul(
                            ps[bank][:],
                            sb[wt][:, k * I + m * 128:k * I + (m + 1) * 128],
                            sb["xq"][:, xsl[0] + k * 512:xsl[0] + (k + 1) * 512],
                            start=(k == 0), stop=(k == KC - 1))
                        if k == KC - 1:
                            ins.then_inc(sems["s_pe"], 1)
                self.on("tensor", mm)
                self.marks[(wname, "mm", g, m)] = self.bump("s_pe", 1)

                slot = m % 2
                if m >= 2:
                    pslot = (g, m - 2)
                elif g >= 1:
                    pslot = (g - 1, IC - 2 + m)
                else:
                    pslot = None
                spw = self.marks.get((wname, "sp") + pslot) if pslot else None
                mxw = self.marks.get((wname, "mx") + pslot) if pslot else None
                def ev(scalar, bank=bank, slot=slot,
                       pv=self.marks[(wname, "mm", g, m)], spw=spw, mxw=mxw):
                    scalar.wait_ge(sems["s_pe"], pv)
                    if spw is not None:
                        scalar.wait_ge(sems["s_sp"], spw)
                    if mxw is not None:
                        scalar.wait_ge(sems["s_dve"], mxw)
                    scalar.copy(sb["f1e"][:, slot * 512:(slot + 1) * 512],
                                ps[bank][:]).then_inc(sems["s_act"], 1)
                self.on("scalar", ev)
                mm_ev[(g, m)] = self.bump("s_act", 1)
                self.marks[(wname, "ev2", g, m)] = mm_ev[(g, m)]

                def sp(sync, g=g, m=m, slot=slot, spill=spill, ev_=mm_ev[(g, m)]):
                    sync.wait_ge(sems["s_act"], ev_)
                    sync.dma_start(out=spill[m * 128:(m + 1) * 128, g * 512:(g + 1) * 512],
                                   in_=sb["f1e"][:, slot * 512:(slot + 1) * 512]).then_inc(sems["s_sp"], 16)
                self.on("sync", sp)
                self.marks[(wname, "sp", g, m)] = self.bump("s_sp", 16)

                def mx(vector, g=g, m=m, slot=slot, gx0=gx0, ev_=mm_ev[(g, m)]):
                    vector.wait_ge(sems["s_act"], ev_)
                    vector.reduce_max(sb["st_gmax"][:, gx0 + g * IC + m:gx0 + g * IC + m + 1],
                                      sb["f1e"][:, slot * 512:(slot + 1) * 512],
                                      AX, apply_absolute_value=True).then_inc(sems["s_dve"], 1)
                self.on("vector", mx)
                self.marks[(wname, "mx", g, m)] = self.bump("s_dve", 1)

        last_mx = self.marks[(wname, "mx", 3, IC - 1)]
        col = 1 if is_gate else 2
        def gm(vector, last_mx=last_mx, gx0=gx0, col=col):
            vector.wait_ge(sems["s_dve"], last_mx)
            vector.reduce_max(sb["sca"][:, col:col + 1],
                              sb["st_gmax"][:, gx0:gx0 + 96], AX).then_inc(sems["s_dve"], 1)
        self.on("vector", gm)
        gmv = self.bump("s_dve", 1)
        tag = "ar2" if is_gate else "ar5"
        fam = "q_sg" if is_gate else "q_su"
        self.ar_chain(tag, sb["sca"][:, col:col + 1], [sb[fam][:, 0:1]], ("s_dve", gmv))
        self.scale_chain(fam, QMAX_A, self.marks[tag + "_done"])
        self.marks["f1_done_" + wname] = self.marks[fam + "_ready"]

    def p1_debug(self):
        sb, sems = self.sb, self.sems
        wg = self.marks["f1_done_w_gate"]
        wu = self.marks["f1_done_w_up"]
        sp_all = self.cur("s_sp")
        def dmp(sync, wg=wg, wu=wu, sp_all=sp_all):
            sync.wait_ge(sems["s_dve"], max(wg, wu))
            sync.wait_ge(sems["s_sp"], sp_all)
            sync.dma_start(out=self.dram["d_graw"],
                           in_=self.dram["spill_g"][0:128, 0:2048]).then_inc(sems["s_out"], 16)
            sync.dma_start(out=self.dram["d_uraw"],
                           in_=self.dram["spill_u"][0:128, 0:2048]).then_inc(sems["s_out"], 16)
            sync.dma_start(out=self.dram["d_wgq"],
                           in_=sb["wgT"][:, 0:I]).then_inc(sems["s_out"], 16)
            sync.dma_start(out=self.dram["d_sca2"], in_=sb["q_sg"][:, 0:24]).then_inc(sems["s_out"], 16)
            sync.dma_start(out=self.dram["d_sca3"], in_=sb["q_su"][:, 0:24]).then_inc(sems["s_out"], 16)
        self.on("sync", dmp)
        self.bump("s_out", 16 * 5)

    def p2(self):
        S, sems = self.S, self.sems
        S("k_grp", (128, KC * 512), BF16)
        S("v_grp", (128, 4 * 1024), BF16)
        S("kt", (128, 1024), BF16)
        S("kv_sb", (64, H * D), F32)
        S("kv_tmp", (64, H * D), F32)
        S("kv_tmp2", (64, H * D), F32)
        S("kv_bf", (64, H * D), BF16)
        S("kv_full", (128, H * D), BF16)
        with ExitStack() as es2a:
            outer = self.cur_es
            self.cur_es = es2a
            S("wk_s", (128, KC * E), BF16)
            S("wv_s", (128, KC * E), BF16)
            S("prefa", (64, H * D), F32)
            S("prefb", (64, H * D), F32)
            S("prefw_a", (64, H * D), F32)
            S("prefw_b", (64, H * D), F32)
            self.p2a()
            if int(os.environ.get("KP2LVL", "3")) >= 1:
                self.p2_prefix()
            self.cur_es = outer
        with ExitStack() as es2b:
            outer = self.cur_es
            self.cur_es = es2b
            S("wq_s", (128, KC * E), BF16)
            S("wo_s", (128, KC * E), BF16)
            S("wg1_s", (128, KC * D), BF16)
            S("wg2_s", (64, E), BF16)
            S("q_grp", (128, KC * 512), BF16)
            S("qh_pk", (128, 1024), BF16)
            S("scb", (128, 2 * 512), BF16)
            S("o3_sb", (64, 4 * 512), BF16)
            S("o_sb64", (64, H * 128), BF16)
            S("o_tm", (128, E), BF16)
            S("oln", (128, 1024), F32)
            S("gb_bc", (128, 2 * E), F32)
            S("lnst", (128, 64), F32)
            S("g1_sb", (64, 128), BF16)
            S("gate_tm", (128, 1024), BF16)
            S("og_fm", (128, 1024), BF16)
            S("f2in", (128, 2 * 1024), BF16)
            S("f2A", (128, 1024), F32)
            S("f2B", (128, 1024), F32)
            S("f2C", (128, 1024), F32)
            S("f2t", (128, 1024), BF16)
            if int(os.environ.get("KP2LVL", "3")) >= 2:
                self.p2b()
                if int(os.environ.get("KP2LVL", "3")) >= 3:
                    self.p2_f2b()
            self.marks["p2_all"] = dict(self.counts)
            self.cur_es = outer

    def p2a(self):
        sb, ps, sems = self.sb, self.ps, self.sems
        f1done = dict(self.counts)
        def wload(gpsimd, a=f1done["s_act"], d=f1done["s_dve"], p=f1done["s_pe"],
                  sp=f1done["s_sp"]):
            gpsimd.wait_ge(sems["s_act"], a)
            gpsimd.wait_ge(sems["s_dve"], d)
            gpsimd.wait_ge(sems["s_pe"], p)
            gpsimd.wait_ge(sems["s_sp"], sp)
            gpsimd.dma_start(out=sb["wk_s"][:].rearrange("p (k e) -> p k e", e=E),
                             in_=self.dram["wk"].rearrange("(k p) e -> p k e", p=128)).then_inc(sems["s_w"], 16)
            gpsimd.dma_start(out=sb["wv_s"][:].rearrange("p (k e) -> p k e", e=E),
                             in_=self.dram["wv"].rearrange("(k p) e -> p k e", p=128)).then_inc(sems["s_w"], 16)
        self.on("gpsimd", wload)
        wkv = self.bump("s_w", 32)

        def kvz(vector, w=f1done["s_dve"]):
            vector.memset(sb["kv_sb"][:], 0.0).then_inc(sems["s_dve"], 1)
        self.on("vector", kvz)
        kv_ok = self.bump("s_dve", 1)
        self.marks["p2a_kv"] = kv_ok

        for g in range(4):
            ev_k = {}
            for m in range(KC):
                bank = "pb%d" % (2 + m % 2)
                mw = [("s_w", wkv)]
                if m >= 2:
                    mw.append(("s_act", ev_k[m - 2]))
                elif g >= 1:
                    mw.append(("s_act", self.marks[("p2a_kev", g - 1, KC - 2 + m)]))
                def mm(tensor, g=g, m=m, bank=bank, mw=tuple(mw)):
                    for s_, v_ in mw:
                        tensor.wait_ge(sems[s_], v_)
                    for k in range(KC):
                        ins = tensor.matmul(
                            ps[bank][:],
                            sb["wk_s"][:, k * E + m * 128:k * E + (m + 1) * 128],
                            sb["xn_fm"][:, k * T + g * 512:k * T + (g + 1) * 512],
                            start=(k == 0), stop=(k == KC - 1))
                        if k == KC - 1:
                            ins.then_inc(sems["s_pe"], 1)
                self.on("tensor", mm)
                pv = self.bump("s_pe", 1)

                ew = []
                if g >= 1 and m < 2:
                    ew.append(("s_sp", self.marks[("p2a_ksp", g - 1)]))
                def ev(scalar, g=g, m=m, bank=bank, pv=pv, ew=tuple(ew)):
                    scalar.wait_ge(sems["s_pe"], pv)
                    for s_, v_ in ew:
                        scalar.wait_ge(sems[s_], v_)
                    scalar.activation(sb["k_grp"][:, m * 512:(m + 1) * 512],
                                      ps[bank][:], AF.Silu).then_inc(sems["s_act"], 1)
                self.on("scalar", ev)
                ev_k[m] = self.bump("s_act", 1)
                self.marks[("p2a_kev", g, m)] = ev_k[m]

            def spk(sync, g=g, last=ev_k[KC - 1]):
                sync.wait_ge(sems["s_act"], last)
                dst = self.dram["spill_k"].rearrange("p (m t) -> p m t", t=T)[
                    :, :, g * 512:(g + 1) * 512]
                src = sb["k_grp"][:].rearrange("p (m t) -> p m t", t=512)
                sync.dma_start(out=dst, in_=src).then_inc(sems["s_sp"], 16)
            self.on("sync", spk)
            self.marks[("p2a_ksp", g)] = self.bump("s_sp", 16)

            for cc in range(4):
                ci = g * 4 + cc
                vw = [("s_w", wkv)]
                if ci >= 1:
                    vw.append(("s_act", self.marks[("p2a_vev", ci - 1)]))
                if g >= 1 and cc == 0:
                    vw.append(("s_sp", self.marks[("p2a_vsp", g - 1)]))
                def mmv(tensor, g=g, cc=cc, ci=ci, vw=tuple(vw)):
                    for s_, v_ in vw:
                        tensor.wait_ge(sems[s_], v_)
                    for k in range(KC):
                        i1 = tensor.matmul(
                            ps["pb4"][:],
                            sb["xn_fm"][:, k * T + ci * 128:k * T + (ci + 1) * 128],
                            sb["wv_s"][:, k * E:k * E + 512],
                            start=(k == 0), stop=(k == KC - 1))
                        i2 = tensor.matmul(
                            ps["pb5"][:],
                            sb["xn_fm"][:, k * T + ci * 128:k * T + (ci + 1) * 128],
                            sb["wv_s"][:, k * E + 512:(k + 1) * E],
                            start=(k == 0), stop=(k == KC - 1))
                        if k == KC - 1:
                            i2.then_inc(sems["s_pe"], 1)
                self.on("tensor", mmv)
                pv = self.bump("s_pe", 1)

                def evv(scalar, cc=cc, pv=pv):
                    scalar.wait_ge(sems["s_pe"], pv)
                    scalar.activation(sb["v_grp"][:, cc * 1024:cc * 1024 + 512],
                                      ps["pb4"][:], AF.Silu)
                    scalar.activation(sb["v_grp"][:, cc * 1024 + 512:(cc + 1) * 1024],
                                      ps["pb5"][:], AF.Silu).then_inc(sems["s_act"], 1)
                self.on("scalar", evv)
                self.marks[("p2a_vev", ci)] = self.bump("s_act", 1)

                self.emit_ktr(g, cc, ci, "p2a")
                if self.debug and ci == 0:
                    def dmpk(sync, w=self.ktr_ev[("p2a", 0)]):
                        sync.wait_ge(sems["s_dve"], w)
                        src = sb["k_grp"][:].rearrange("p (m t) -> p m t", t=512)[:, :, 0:128]
                        sync.dma_start(out=self.dram["d_k"].rearrange("p (m t) -> p m t", t=128),
                                       in_=src).then_inc(sems["s_out"], 16)
                        sync.dma_start(out=self.dram["d_q"], in_=sb["kt"][:]).then_inc(sems["s_out"], 16)
                    self.on("sync", dmpk)
                    self.marks["dbg_k0"] = self.bump("s_out", 32)
                self.emit_kv_chain(g, cc, ci, first=(ci == 0), tag="p2a")

            def spv(sync, g=g, last=self.marks[("p2a_vev", g * 4 + 3)]):
                sync.wait_ge(sems["s_act"], last)
                sync.dma_start(out=self.dram["spill_v"][:, g * 4096:(g + 1) * 4096],
                               in_=sb["v_grp"][:]).then_inc(sems["s_sp"], 16)
            self.on("sync", spv)
            self.marks[("p2a_vsp", g)] = self.bump("s_sp", 16)

        last_kv = self.marks[("p2a_kvup", 15)]
        def sps(sync, last_kv=last_kv):
            sync.wait_ge(sems["s_dve"], last_kv)
            sync.dma_start(out=self.dram["cc_agi"], in_=sb["kv_sb"][:]).then_inc(sems["s_st"], 16)
        self.on("sync", sps)
        spsv = self.bump("s_st", 16)

        def ag(gpsimd, spsv=spsv):
            gpsimd.wait_ge(sems["s_st"], spsv)
            gpsimd.collective_compute("AllGather", ALU.bypass,
                                      replica_groups=[list(range(N_CORES))],
                                      ins=[self.dram["cc_agi"]],
                                      outs=[self.dram["cc_ago"]]).then_inc(sems["c_ag"], 1)
        self.on("gpsimd", ag)
        self.bump("c_ag", 1)

    def emit_ktr(self, g, cc, ci, tag):
        sb, ps, sems = self.sb, self.ps, self.sems
        bank = "pb%d" % (ci % 2)
        pw = []
        if not hasattr(self, "ktr_ev"):
            self.ktr_ev = {}
        if (tag, ci - 2) in self.ktr_ev:
            pw.append(("s_dve", self.ktr_ev[(tag, ci - 2)]))
        kev = self.marks[("p2a_kev", g, KC - 1)] if tag == "p2a" else self.marks[("p2b_kld", g)]
        if (tag, ci - 1) in getattr(self, "mm4_done", {}):
            pw.append(("s_pe", self.mm4_done[(tag, ci - 1)]))
        if tag == "p2b":
            p_ = self.marks.get(("p2b_ogfm", ci - 1))
            if p_ is not None:
                pw.append(("s_act", p_))
        if self.debug and tag == "p2a" and ci == 1 and "dbg_k0" in self.marks:
            pw.append(("s_out", self.marks["dbg_k0"]))
        def tr(tensor, g=g, cc=cc, bank=bank, kev=kev, pw=tuple(pw)):
            tensor.wait_ge(sems["s_act"], kev)
            for s_, v_ in pw:
                tensor.wait_ge(sems[s_], v_)
            pt = ps[bank][:].bitcast(BF16)
            for m in range(KC):
                ins = tensor.transpose(
                    pt[:, m * 128:(m + 1) * 128],
                    sb["k_grp"][:, m * 512 + cc * 128:m * 512 + (cc + 1) * 128],
                    sb["idents"][:])
                if m == KC - 1:
                    ins.then_inc(sems["s_pe"], 1)
        self.on("tensor", tr)
        pv = self.bump("s_pe", 1)

        def ev(vector, bank=bank, pv=pv):
            vector.wait_ge(sems["s_pe"], pv)
            pt = ps[bank][:].bitcast(BF16)
            vector.tensor_tensor(sb["kt"][:], pt[:], sb["lamks"][:],
                                 ALU.mult).then_inc(sems["s_dve"], 1)
        self.on("vector", ev)
        self.ktr_ev[(tag, ci)] = self.bump("s_dve", 1)

    def emit_kv_chain(self, g, cc, ci, first, tag):
        sb, ps, sems = self.sb, self.ps, self.sems
        if not hasattr(self, "mm4_done"):
            self.mm4_done = {}
        ktv = self.ktr_ev[(tag, ci)]
        vev = self.marks[("p2a_vev", ci)] if tag == "p2a" else self.marks[("p2b_vld", g)]
        pw = []
        if (tag, ci - 1) in self.mm4_done or not first:
            prev = self.marks.get((tag + "_kvup", ci - 1))
            if prev is not None:
                pw.append(("s_dve", prev))
        if tag == "p2b":
            prev_wo = self.marks.get(("p2b_woev", ci - 1))
            if prev_wo is not None:
                pw.append(("s_dve", prev_wo))
        def mm4(tensor, g=g, cc=cc, ktv=ktv, vev=vev, pw=tuple(pw)):
            tensor.wait_ge(sems["s_dve"], ktv)
            tensor.wait_ge(sems["s_act"], vev)
            for s_, v_ in pw:
                tensor.wait_ge(sems[s_], v_)
            for h in range(H):
                bank = "pb6" if h < 8 else "pb7"
                col = (h % 8) * 64
                ins = tensor.matmul(
                    ps[bank][0:64, col:col + 64],
                    sb["kt"][:, (h // 2) * 128 + (h % 2) * 64:(h // 2) * 128 + (h % 2) * 64 + 64],
                    sb["v_grp"][:, cc * 1024 + h * 64:cc * 1024 + (h + 1) * 64],
                    start=True, stop=True)
                if h == H - 1:
                    ins.then_inc(sems["s_pe"], 1)
        self.on("tensor", mm4)
        self.mm4_done[(tag, ci)] = self.bump("s_pe", 1)

        dbgk = self.marks.get("dbg_pref") if (self.debug and tag == "p2b" and ci == 0) else None
        def upd(vector, first=first, pv=self.mm4_done[(tag, ci)], base=self.cur("s_dve"),
                dbgk=dbgk):
            v = vector
            v.wait_ge(sems["s_pe"], pv)
            if dbgk is not None:
                v.wait_ge(sems["s_out"], dbgk)
            n = 0
            for half in range(2):
                bank = "pb6" if half == 0 else "pb7"
                colr = slice(half * 512, (half + 1) * 512)
                if first:
                    v.tensor_copy(sb["kv_tmp"][:, colr], ps[bank][0:64, 0:512]).then_inc(sems["s_dve"], 1)
                    n += 1
                else:
                    v.tensor_tensor(sb["kv_tmp2"][:, colr], sb["kv_sb"][:, colr],
                                    sb["lamcs"][:, colr], ALU.mult).then_inc(sems["s_dve"], 1)
                    n += 1
                    v.wait_ge(sems["s_dve"], base + n)
                    v.tensor_tensor(sb["kv_tmp"][:, colr], sb["kv_tmp2"][:, colr],
                                    ps[bank][0:64, 0:512], ALU.add).then_inc(sems["s_dve"], 1)
                    n += 1
            v.wait_ge(sems["s_dve"], base + n)
            v.tensor_copy(sb["kv_sb"][:], sb["kv_tmp"][:]).then_inc(sems["s_dve"], 1)
        nops = 3 if first else 5
        self.on("vector", upd)
        self.bump("s_dve", nops - 1)
        self.marks[(tag + "_kvup", ci)] = self.bump("s_dve", 1)

    def p2_prefix(self):
        sb, ps, sems = self.sb, self.ps, self.sems
        ldm = {}
        for cp in range(8):
            def ld(sync, cp=cp, buf=cp % 2):
                sync.wait_ge(sems["c_ag"], 1)
                if cp >= 2:
                    sync.wait_ge(sems["s_dve"], ldm[cp - 2])
                a = "prefa" if buf == 0 else "prefb"
                w = "prefw_a" if buf == 0 else "prefw_b"
                sync.dma_start(out=sb[a][:],
                               in_=self.dram["cc_ago"][cp * 64:(cp + 1) * 64, :]).then_inc(sems["s_st"], 16)
                sync.dma_start(out=sb[w][:],
                               in_=self.dram["prefw"][cp * 64:(cp + 1) * 64, :]).then_inc(sems["s_st"], 16)
            self.on("sync", ld)
            ldv = self.bump("s_st", 32)

            def acc(vector, cp=cp, buf=cp % 2, ldv=ldv, base=self.cur("s_dve")):
                v = vector
                v.wait_ge(sems["s_st"], ldv)
                a = "prefa" if buf == 0 else "prefb"
                w = "prefw_a" if buf == 0 else "prefw_b"
                if cp == 0:
                    v.tensor_tensor(sb["kv_sb"][:], sb[a][:], sb[w][:], ALU.mult).then_inc(sems["s_dve"], 1)
                else:
                    v.tensor_tensor(sb["kv_tmp"][:], sb[a][:], sb[w][:], ALU.mult).then_inc(sems["s_dve"], 1)
                    v.wait_ge(sems["s_dve"], base + 1)
                    v.tensor_tensor(sb["kv_sb"][:], sb["kv_sb"][:], sb["kv_tmp"][:],
                                    ALU.add).then_inc(sems["s_dve"], 1)
            n = 1 if cp == 0 else 2
            self.on("vector", acc)
            self.bump("s_dve", n - 1)
            ldm[cp] = self.bump("s_dve", 1)
        if self.debug:
            def dmpS(sync, w=ldm[7]):
                sync.wait_ge(sems["s_dve"], w)
                sync.dma_start(out=self.dram["d_S"], in_=self.dram["cc_agi"]).then_inc(sems["s_out"], 16)
                sync.dma_start(out=self.dram["d_kvin"], in_=sb["kv_sb"][:]).then_inc(sems["s_out"], 16)
            self.on("sync", dmpS)
            self.marks["dbg_pref"] = self.bump("s_out", 32)
        self.emit_kv_full("pref", ldm[7])

    def emit_kv_full(self, tag, kv_mark):
        sb, ps, sems = self.sb, self.ps, self.sems
        def bf(vector, kv_mark=kv_mark):
            vector.wait_ge(sems["s_dve"], kv_mark)
            vector.tensor_copy(sb["kv_bf"][:], sb["kv_sb"][:]).then_inc(sems["s_dve"], 1)
        self.on("vector", bf)
        bfv = self.bump("s_dve", 1)

        def jmm(tensor, bfv=bfv):
            tensor.wait_ge(sems["s_dve"], bfv)
            tensor.matmul(ps["pb6"][:], sb["j2s"][:], sb["kv_bf"][:, 0:512],
                          start=True, stop=True)
            tensor.matmul(ps["pb7"][:], sb["j2s"][:], sb["kv_bf"][:, 512:1024],
                          start=True, stop=True).then_inc(sems["s_pe"], 1)
        self.on("tensor", jmm)
        jv = self.bump("s_pe", 1)

        def jev(scalar, jv=jv):
            scalar.wait_ge(sems["s_pe"], jv)
            scalar.copy(sb["kv_full"][:, 0:512], ps["pb6"][:])
            scalar.copy(sb["kv_full"][:, 512:1024], ps["pb7"][:]).then_inc(sems["s_act"], 1)
        self.on("scalar", jev)
        self.marks[(tag, "kvfull")] = self.bump("s_act", 1)
        self.marks["kvfull_latest"] = self.marks[(tag, "kvfull")]

    def p2b(self):
        sb, ps, sems = self.sb, self.ps, self.sems
        pref_done = self.marks[("pref", "kvfull")]
        def wload(gpsimd, w=pref_done, d=self.cur("s_dve"), st=self.cur("s_st"),
                  pe=self.cur("s_pe")):
            gpsimd.wait_ge(sems["s_act"], w)
            gpsimd.wait_ge(sems["s_dve"], d)
            gpsimd.wait_ge(sems["s_st"], st)
            gpsimd.wait_ge(sems["s_pe"], pe)
            gpsimd.dma_start(out=sb["wq_s"][:].rearrange("p (k e) -> p k e", e=E),
                             in_=self.dram["wq"].rearrange("(k p) e -> p k e", p=128)).then_inc(sems["s_w"], 16)
            gpsimd.dma_start(out=sb["wo_s"][:].rearrange("p (k e) -> p k e", e=E),
                             in_=self.dram["wo"].rearrange("(k p) e -> p k e", p=128)).then_inc(sems["s_w"], 16)
            gpsimd.dma_start(out=sb["wg1_s"][:].rearrange("p (k j) -> p k j", j=D),
                             in_=self.dram["wg1"].rearrange("(k p) j -> p k j", p=128)).then_inc(sems["s_w"], 16)
            gpsimd.dma_start(out=sb["wg2_s"][:], in_=self.dram["wg2"]).then_inc(sems["s_w"], 16)
        self.on("gpsimd", wload)
        wq_ok = self.bump("s_w", 64)

        def gbmm(tensor, w=wq_ok, cv=self.marks["consts"]):
            tensor.wait_ge(sems["s_w"], w)
            tensor.wait_ge(sems["s_x"], cv)
            tensor.matmul(ps["pb2"][:], sb["oness"][:], sb["gam_row"][0:1, 0:512],
                          start=True, stop=True)
            tensor.matmul(ps["pb3"][:], sb["oness"][:], sb["gam_row"][0:1, 512:1024],
                          start=True, stop=True)
            tensor.matmul(ps["pb4"][:], sb["oness"][:], sb["bet_row"][0:1, 0:512],
                          start=True, stop=True)
            ins = tensor.matmul(ps["pb5"][:], sb["oness"][:], sb["bet_row"][0:1, 512:1024],
                                start=True, stop=True)
            ins.then_inc(sems["s_pe"], 1)
        self.on("tensor", gbmm)
        gbv = self.bump("s_pe", 1)
        def gbev(scalar, gbv=gbv):
            scalar.wait_ge(sems["s_pe"], gbv)
            scalar.copy(sb["gb_bc"][:, 0:512], ps["pb2"][:])
            scalar.copy(sb["gb_bc"][:, 512:1024], ps["pb3"][:])
            scalar.copy(sb["gb_bc"][:, 1024:1536], ps["pb4"][:])
            scalar.copy(sb["gb_bc"][:, 1536:2048], ps["pb5"][:]).then_inc(sems["s_act"], 1)
        self.on("scalar", gbev)
        self.marks["gb_ready"] = self.bump("s_act", 1)

        f2a_idx = 0
        for g in range(4):
            ev_q = {}
            for m in range(KC):
                bank = "pb%d" % (2 + m % 2)
                mw = [("s_w", wq_ok)]
                if m >= 2:
                    mw.append(("s_act", ev_q[m - 2]))
                elif g >= 1:
                    mw.append(("s_act", self.marks[("p2b_qev", g - 1, KC - 2 + m)]))
                    if ("p2b_mm3last", g * 4 - 1) in self.marks:
                        mw.append(("s_pe", self.marks[("p2b_mm3last", g * 4 - 1)]))
                    if ("p2b_qh", g * 4 - 1) in self.marks:
                        mw.append(("s_dve", self.marks[("p2b_qh", g * 4 - 1)]))
                    if ("p2b_g2e", g * 4 - 1) in self.marks:
                        mw.append(("s_act", self.marks[("p2b_g2e", g * 4 - 1)]))
                else:
                    mw.append(("s_act", self.marks[("pref", "kvfull")]))
                    mw.append(("s_act", self.marks["gb_ready"]))
                def mm(tensor, g=g, m=m, bank=bank, mw=tuple(mw)):
                    for s_, v_ in mw:
                        tensor.wait_ge(sems[s_], v_)
                    for k in range(KC):
                        ins = tensor.matmul(
                            ps[bank][:],
                            sb["wq_s"][:, k * E + m * 128:k * E + (m + 1) * 128],
                            sb["xn_fm"][:, k * T + g * 512:k * T + (g + 1) * 512],
                            start=(k == 0), stop=(k == KC - 1))
                        if k == KC - 1:
                            ins.then_inc(sems["s_pe"], 1)
                self.on("tensor", mm)
                pv = self.bump("s_pe", 1)
                def ev(scalar, m=m, bank=bank, pv=pv):
                    scalar.wait_ge(sems["s_pe"], pv)
                    scalar.activation(sb["q_grp"][:, m * 512:(m + 1) * 512],
                                      ps[bank][:], AF.Silu).then_inc(sems["s_act"], 1)
                self.on("scalar", ev)
                ev_q[m] = self.bump("s_act", 1)
                self.marks[("p2b_qev", g, m)] = ev_q[m]

            def kld(sync, g=g, kq=ev_q[KC - 1],
                    sp=self.marks[("p2a_ksp", 3)],
                    prevtr=self.marks.get(("p2b_mm3last", g * 4 - 1))):
                sync.wait_ge(sems["s_sp"], sp)
                if prevtr is not None:
                    sync.wait_ge(sems["s_pe"], prevtr)
                src = self.dram["spill_k"].rearrange("p (m t) -> p m t", t=T)[
                    :, :, g * 512:(g + 1) * 512]
                sync.dma_start(out=sb["k_grp"][:].rearrange("p (m t) -> p m t", t=512),
                               in_=src).then_inc(sems["s_st"], 16)
            self.on("sync", kld)
            self.marks[("p2b_kld_st", g)] = self.bump("s_st", 16)
            dbgv = self.marks.get("dbg_ep0") if (self.debug and g == 1) else None
            def vld(sync, g=g, sp=self.marks[("p2a_vsp", 3)],
                    prevmm=self.mm4_done.get(("p2b", g * 4 - 1)), dbgv=dbgv):
                sync.wait_ge(sems["s_sp"], sp)
                if prevmm is not None:
                    sync.wait_ge(sems["s_pe"], prevmm)
                if dbgv is not None:
                    sync.wait_ge(sems["s_out"], dbgv)
                sync.dma_start(out=sb["v_grp"][:],
                               in_=self.dram["spill_v"][:, g * 4096:(g + 1) * 4096]).then_inc(sems["s_st"], 16)
            self.on("sync", vld)
            self.marks[("p2b_vld_st", g)] = self.bump("s_st", 16)
            def ready(vector, a=self.marks[("p2b_kld_st", g)], b=self.marks[("p2b_vld_st", g)]):
                vector.wait_ge(sems["s_st"], a)
                vector.wait_ge(sems["s_st"], b)
                vector.engine_nop().then_inc(sems["s_dve"], 1)
            self.on("vector", ready)
            rd = self.bump("s_dve", 1)
            def readys(scalar, rd=rd):
                scalar.wait_ge(sems["s_dve"], rd)
                scalar.nop().then_inc(sems["s_act"], 1)
            self.on("scalar", readys)
            br = self.bump("s_act", 1)
            self.marks[("p2b_kld", g)] = br
            self.marks[("p2b_vld", g)] = br

            for cc in range(4):
                ci = g * 4 + cc
                if not int(os.environ.get("KCHUNKS", "1")):
                    continue
                self.emit_ktr(g, cc, ci, "p2b")
                self.p2b_scan_chunk(g, cc, ci)
                if int(os.environ.get("KKV", "1")):
                    self.emit_kv_chain(g, cc, ci, first=False, tag="p2b")
                    self.emit_kv_full("c%d" % ci, self.marks[("p2b_kvup", ci)])
                if int(os.environ.get("KEPI", "1")):
                    self.p2b_epilogue(g, cc, ci)
                if int(os.environ.get("KF2A", "1")):
                    for _ in range(3):
                        self.emit_f2a_tile(f2a_idx)
                        f2a_idx += 1
        self.marks["p2b_done"] = dict(self.counts)

    def p2b_scan_chunk(self, g, cc, ci):
        sb, ps, sems = self.sb, self.ps, self.sems
        ktv = self.ktr_ev[("p2b", ci)]
        kvf = self.marks["kvfull_latest"]
        qev = self.marks[("p2b_qev", g, KC - 1)]
        qhw = []
        prev_mm3 = self.marks.get(("p2b_mm3last", ci - 1))
        if prev_mm3 is not None:
            qhw.append(("s_pe", prev_mm3))
        LVL0 = int(os.environ.get("KSCANLVL", "3"))
        if LVL0 < 1:
            self.marks[("p2b_mm3last", ci)] = self.cur("s_pe")
            self.marks[("p2b_oev", ci)] = self.cur("s_dve")
            self.marks[("p2b_o3ev", ci)] = self.cur("s_act")
            self.marks[("p2b_qh", ci)] = self.cur("s_dve")
            return
        def qh(vector, g=g, cc=cc, qev=qev, qhw=tuple(qhw)):
            vector.wait_ge(sems["s_act"], qev)
            for s_, v_ in qhw:
                vector.wait_ge(sems[s_], v_)
            for m in range(KC):
                ins = vector.tensor_tensor(
                    sb["qh_pk"][:, m * 128:(m + 1) * 128],
                    sb["q_grp"][:, m * 512 + cc * 128:m * 512 + (cc + 1) * 128],
                    sb["lamqs"][:, m * 128:(m + 1) * 128], ALU.mult)
                if m == KC - 1:
                    ins.then_inc(sems["s_dve"], 1)
        self.on("vector", qh)
        qhv = self.bump("s_dve", 1)
        self.marks[("p2b_qh", ci)] = qhv

        LVL = int(os.environ.get("KSCANLVL", "3"))
        for hg in range(4):
            obank = "pb%d" % (4 + hg % 2)
            if LVL < 2:
                continue
            sw = [("s_act", qev), ("s_dve", qhv)]
            if hg >= 1:
                sw.append(("s_act", self.marks[("p2b_o3evhg", ci, hg - 1)]))
            else:
                p_ = self.marks.get(("p2b_g2e", ci - 1))
                sw.append(("s_act", p_ if p_ is not None else self.marks["gb_ready"]))
                p2_ = self.marks.get(("p2b_o3evhg", ci - 1, 3))
                if p2_ is not None:
                    sw.append(("s_act", p2_))
            def mm1(tensor, g=g, cc=cc, hg=hg, sw=tuple(sw), ktv=ktv):
                tensor.wait_ge(sems["s_dve"], ktv)
                for s_, v_ in sw:
                    tensor.wait_ge(sems[s_], v_)
                for par in range(2):
                    bank = "pb2" if par == 0 else "pb3"
                    pb = par * 64
                    for jj in range(2):
                        h = hg * 4 + 2 * jj + par
                        mcol = (h // 2) * 512 + cc * 128
                        ins = tensor.matmul(
                            ps[bank][:, jj * 128:(jj + 1) * 128],
                            sb["k_grp"][pb:pb + 64, mcol:mcol + 128],
                            sb["q_grp"][pb:pb + 64, mcol:mcol + 128],
                            start=True, stop=True)
                        if par == 1 and jj == 1:
                            ins.then_inc(sems["s_pe"], 1)
            self.on("tensor", mm1)
            mm1v = self.bump("s_pe", 1)

            def msk(vector, hg=hg, mm1v=mm1v):
                vector.wait_ge(sems["s_pe"], mm1v)
                vector.tensor_tensor(sb["scb"][:, 0:256], ps["pb2"][:, 0:256],
                                     sb["maskTs"][:, hg * 512:hg * 512 + 256], ALU.mult)
                vector.tensor_tensor(sb["scb"][:, 256:512], ps["pb3"][:, 0:256],
                                     sb["maskTs"][:, hg * 512 + 256:(hg + 1) * 512],
                                     ALU.mult).then_inc(sems["s_dve"], 1)
            self.on("vector", msk)
            mskv = self.bump("s_dve", 1)

            if LVL < 3:
                self.marks[("p2b_o3evhg", ci, hg)] = self.cur("s_act")
                self.marks[("p2b_oevhg", ci, hg)] = mskv
                self.marks[("p2b_mm3hg", ci, hg)] = mm1v
                continue
            gbw = self.marks["gb_ready"] if ci == 0 else None
            def mm23(tensor, g=g, cc=cc, hg=hg, obank=obank,
                     mskv=mskv, kvf=kvf, gbw=gbw):
                tensor.wait_ge(sems["s_dve"], mskv)
                tensor.wait_ge(sems["s_act"], kvf)
                if gbw is not None:
                    tensor.wait_ge(sems["s_act"], gbw)
                for par in range(2):
                    for jj in range(2):
                        h = hg * 4 + 2 * jj + par
                        col = par * 256 + jj * 128
                        tensor.matmul(
                            ps[obank][0:64, col:col + 128],
                            sb["v_grp"][:, cc * 1024 + h * 64:cc * 1024 + (h + 1) * 64],
                            sb["scb"][:, col:col + 128],
                            start=True, stop=True)
                for par in range(2):
                    bank = "pb2" if par == 0 else "pb3"
                    pb = par * 64
                    for jj in range(2):
                        h = hg * 4 + 2 * jj + par
                        ins = tensor.matmul(
                            ps[bank][0:64, jj * 128:(jj + 1) * 128],
                            sb["kv_full"][pb:pb + 64, h * 64:(h + 1) * 64],
                            sb["qh_pk"][pb:pb + 64, (h // 2) * 128:(h // 2 + 1) * 128],
                            start=True, stop=True)
                        if par == 1 and jj == 1:
                            ins.then_inc(sems["s_pe"], 1)
            self.on("tensor", mm23)
            mm23v = self.bump("s_pe", 1)
            self.marks[("p2b_mm3hg", ci, hg)] = mm23v

            o3w = self.marks.get(("p2b_oevhg", ci - 1, hg))
            def o3ev(scalar, hg=hg, mm23v=mm23v, o3w=o3w):
                scalar.wait_ge(sems["s_pe"], mm23v)
                if o3w is not None:
                    scalar.wait_ge(sems["s_dve"], o3w)
                scalar.copy(sb["o3_sb"][:, hg * 512:hg * 512 + 256], ps["pb2"][0:64, 0:256])
                scalar.copy(sb["o3_sb"][:, hg * 512 + 256:(hg + 1) * 512],
                            ps["pb3"][0:64, 0:256]).then_inc(sems["s_act"], 1)
            self.on("scalar", o3ev)
            o3v = self.bump("s_act", 1)
            self.marks[("p2b_o3evhg", ci, hg)] = o3v

            ow = []
            if ci >= 1 and hg < 2:
                p_ = self.marks.get(("p2b_otr", ci - 1))
                if p_ is not None:
                    ow.append(("s_pe", p_))
            def oev(vector, hg=hg, obank=obank, o3v=o3v, ow=tuple(ow)):
                v = vector
                v.wait_ge(sems["s_act"], o3v)
                for s_, v_ in ow:
                    v.wait_ge(sems[s_], v_)
                v.tensor_tensor(sb["o_sb64"][:, hg * 512:(hg + 1) * 512],
                                sb["o3_sb"][:, hg * 512:(hg + 1) * 512],
                                ps[obank][0:64, :], ALU.add).then_inc(sems["s_dve"], 1)
            self.on("vector", oev)
            self.marks[("p2b_oevhg", ci, hg)] = self.bump("s_dve", 1)
        if LVL < 2:
            self.marks[("p2b_mm3last", ci)] = self.cur("s_pe")
            self.marks[("p2b_oev", ci)] = qhv
            self.marks[("p2b_o3ev", ci)] = self.cur("s_act")
        else:
            self.marks[("p2b_mm3last", ci)] = self.marks[("p2b_mm3hg", ci, 3)]
            self.marks[("p2b_oev", ci)] = self.marks[("p2b_oevhg", ci, 3)]
            self.marks[("p2b_o3ev", ci)] = self.marks[("p2b_o3evhg", ci, 3)]

    def p2b_epilogue(self, g, cc, ci):
        sb, ps, sems = self.sb, self.ps, self.sems
        oev = self.marks[("p2b_oev", ci)]
        otrb = "pb%d" % (1 - ci % 2)

        pw = []
        p_ = self.marks.get(("p2b_otm_ev", ci - 1))
        if p_ is not None:
            pw.append(("s_act", p_))
        def otr(tensor, oev=oev, otrb=otrb, pw=tuple(pw)):
            tensor.wait_ge(sems["s_dve"], oev)
            for s_, v_ in pw:
                tensor.wait_ge(sems[s_], v_)
            pt = ps[otrb][:].bitcast(BF16)
            for h in range(H):
                hg_, r = h // 4, h % 4
                scol = hg_ * 512 + (r % 2) * 256 + (r // 2) * 128
                ins = tensor.transpose(pt[:, h * 64:(h + 1) * 64],
                                       sb["o_sb64"][:, scol:scol + 128],
                                       sb["idents"][0:64, 0:64])
                if h == H - 1:
                    ins.then_inc(sems["s_pe"], 1)
        self.on("tensor", otr)
        otrv = self.bump("s_pe", 1)
        self.marks[("p2b_otr", ci)] = otrv

        dbgw = self.marks.get("dbg_ep0") if (self.debug and ci == 1) else None
        def otm(scalar, otrv=otrv, otrb=otrb, dbgw=dbgw):
            scalar.wait_ge(sems["s_pe"], otrv)
            if dbgw is not None:
                scalar.wait_ge(sems["s_out"], dbgw)
            scalar.copy(sb["o_tm"][:], ps[otrb][:].bitcast(BF16)).then_inc(sems["s_act"], 1)
        self.on("scalar", otm)
        otmv = self.bump("s_act", 1)
        self.marks[("p2b_otm_ev", ci)] = otmv

        def osq(scalar, otmv=otmv):
            scalar.wait_ge(sems["s_act"], otmv)
            scalar.activation(sb["sqs"][:], sb["o_tm"][:], AF.Square,
                              accum_out=sb["lnst"][:, 16:17]).then_inc(sems["s_act"], 1)
        self.on("scalar", osq)
        osqv = self.bump("s_act", 1)

        def lns(vector, otmv=otmv, osqv=osqv, base=self.cur("s_dve")):
            v = vector
            v.wait_ge(sems["s_act"], osqv)
            n = 0
            v.reduce_sum(sb["lnst"][:, 0:1], sb["o_tm"][:], AX).then_inc(sems["s_dve"], 1); n += 1
            v.wait_ge(sems["s_dve"], base + n)
            v.tensor_copy(sb["lnst"][:, 0:16], sb["lnst"][:, 0:1].broadcast_to((128, 16))).then_inc(sems["s_dve"], 1); n += 1
            v.wait_ge(sems["s_dve"], base + n)
            v.tensor_copy(sb["lnst"][:, 16:32], sb["lnst"][:, 16:17].broadcast_to((128, 16))).then_inc(sems["s_dve"], 1); n += 1
            v.wait_ge(sems["s_dve"], base + n)
            v.tensor_scalar(sb["lnst"][:, 0:16], sb["lnst"][:, 0:16], 1.0 / E, None, ALU.mult).then_inc(sems["s_dve"], 1); n += 1
            v.tensor_scalar(sb["lnst"][:, 16:32], sb["lnst"][:, 16:32], 1.0 / E, None, ALU.mult).then_inc(sems["s_dve"], 1); n += 1
            v.wait_ge(sems["s_dve"], base + n)
            v.tensor_tensor(sb["lnst"][:, 32:48], sb["lnst"][:, 0:16], sb["lnst"][:, 0:16], ALU.mult).then_inc(sems["s_dve"], 1); n += 1
            v.wait_ge(sems["s_dve"], base + n)
            v.tensor_tensor(sb["lnst"][:, 32:48], sb["lnst"][:, 16:32], sb["lnst"][:, 32:48], ALU.subtract).then_inc(sems["s_dve"], 1); n += 1
        self.on("vector", lns)
        self.bump("s_dve", 6)
        lnsv = self.bump("s_dve", 1)

        def lsq(scalar, lnsv=lnsv):
            scalar.wait_ge(sems["s_dve"], lnsv)
            scalar.activation(sb["lnst"][:, 32:48], sb["lnst"][:, 32:48], AF.Sqrt,
                              bias=sb["kcn"][:, 0:1]).then_inc(sems["s_act"], 1)
        self.on("scalar", lsq)
        lsqv = self.bump("s_act", 1)

        def lns2(vector, lsqv=lsqv, base=self.cur("s_dve")):
            v = vector
            v.wait_ge(sems["s_act"], lsqv)
            n = 0
            v.reciprocal(sb["lnst"][:, 48:64], sb["lnst"][:, 32:48]).then_inc(sems["s_dve"], 1); n += 1
            v.wait_ge(sems["s_dve"], base + n)
            v.tensor_tensor(sb["lnst"][:, 0:16], sb["lnst"][:, 0:16], sb["lnst"][:, 48:64], ALU.mult).then_inc(sems["s_dve"], 1); n += 1
            v.wait_ge(sems["s_dve"], base + n)
            v.tensor_scalar(sb["lnst"][:, 0:16], sb["lnst"][:, 0:16], -1.0, None, ALU.mult).then_inc(sems["s_dve"], 1); n += 1
        self.on("vector", lns2)
        self.bump("s_dve", 2)
        ln2v = self.bump("s_dve", 1)

        lapw = self.marks.get(("p2b_acc", ci - 1))
        def lap(scalar, ln2v=ln2v, gb=self.marks["gb_ready"], lapw=lapw):
            scalar.wait_ge(sems["s_dve"], ln2v)
            scalar.wait_ge(sems["s_act"], gb)
            if lapw is not None:
                scalar.wait_ge(sems["s_out"], lapw)
            scalar.activation(sb["oln"][:], sb["o_tm"][:], AF.Identity,
                              bias=sb["lnst"][:, 0:1],
                              scale=sb["lnst"][:, 48:49]).then_inc(sems["s_act"], 1)
        self.on("scalar", lap)
        lapv = self.bump("s_act", 1)

        def g1mm(tensor, ci=ci, oev=oev):
            tensor.wait_ge(sems["s_dve"], oev)
            for k in range(KC):
                ins = tensor.matmul(ps["pb5"][0:64, 0:128],
                                    sb["wg1_s"][:, k * D:(k + 1) * D],
                                    sb["xn_fm"][:, k * T + ci * 128:k * T + (ci + 1) * 128],
                                    start=(k == 0), stop=(k == KC - 1))
                if k == KC - 1:
                    ins.then_inc(sems["s_pe"], 1)
        self.on("tensor", g1mm)
        g1v = self.bump("s_pe", 1)

        def g1ev(scalar, g1v=g1v):
            scalar.wait_ge(sems["s_pe"], g1v)
            scalar.copy(sb["g1_sb"][:], ps["pb5"][0:64, 0:128]).then_inc(sems["s_act"], 1)
        self.on("scalar", g1ev)
        g1e = self.bump("s_act", 1)

        def g2mm(tensor, g1e=g1e):
            tensor.wait_ge(sems["s_act"], g1e)
            tensor.matmul(ps["pb2"][:], sb["g1_sb"][:], sb["wg2_s"][:, 0:512],
                          start=True, stop=True)
            ins = tensor.matmul(ps["pb3"][:], sb["g1_sb"][:], sb["wg2_s"][:, 512:1024],
                                start=True, stop=True)
            ins.then_inc(sems["s_pe"], 1)
        self.on("tensor", g2mm)
        g2v = self.bump("s_pe", 1)

        def g2ev(scalar, g2v=g2v):
            scalar.wait_ge(sems["s_pe"], g2v)
            scalar.activation(sb["gate_tm"][:, 0:512], ps["pb2"][:], AF.Sigmoid)
            scalar.activation(sb["gate_tm"][:, 512:1024], ps["pb3"][:],
                              AF.Sigmoid).then_inc(sems["s_act"], 1)
        self.on("scalar", g2ev)
        g2e = self.bump("s_act", 1)
        self.marks[("p2b_g2e", ci)] = g2e

        dbgw2 = self.marks.get("dbg_ep0") if (self.debug and ci == 1) else None
        def ogc(vector, lapv=lapv, g2e=g2e, base=self.cur("s_dve"), dbgw2=dbgw2):
            v = vector
            v.wait_ge(sems["s_act"], lapv)
            v.wait_ge(sems["s_act"], g2e)
            if dbgw2 is not None:
                v.wait_ge(sems["s_out"], dbgw2)
            n = 0
            v.tensor_tensor(sb["sqs"][:], sb["oln"][:], sb["gb_bc"][:, 0:1024], ALU.mult).then_inc(sems["s_dve"], 1); n += 1
            v.wait_ge(sems["s_dve"], base + n)
            v.tensor_tensor(sb["sqs"][:], sb["sqs"][:], sb["gb_bc"][:, 1024:2048], ALU.add).then_inc(sems["s_dve"], 1); n += 1
            v.wait_ge(sems["s_dve"], base + n)
            v.tensor_tensor(sb["oln"][:].bitcast(BF16)[:, 0:1024], sb["sqs"][:],
                            sb["gate_tm"][:], ALU.mult).then_inc(sems["s_dve"], 1); n += 1
        self.on("vector", ogc)
        self.bump("s_dve", 2)
        ogv = self.bump("s_dve", 1)
        self.marks[("p2b_ogtt", ci)] = ogv

        def ogtr(tensor, ogv=ogv, otrb=otrb, otmv=otmv):
            tensor.wait_ge(sems["s_dve"], ogv)
            pt = ps[otrb][:].bitcast(BF16)
            for m in range(KC):
                ins = tensor.transpose(pt[:, m * 128:(m + 1) * 128],
                                       sb["oln"][:].bitcast(BF16)[:, m * 128:(m + 1) * 128],
                                       sb["idents"][:])
                if m == KC - 1:
                    ins.then_inc(sems["s_pe"], 1)
        self.on("tensor", ogtr)
        ogtrv = self.bump("s_pe", 1)

        def ogev(scalar, ogtrv=ogtrv, otrb=otrb):
            scalar.wait_ge(sems["s_pe"], ogtrv)
            scalar.copy(sb["og_fm"][:], ps[otrb][:].bitcast(BF16)).then_inc(sems["s_act"], 1)
        self.on("scalar", ogev)
        ogevv = self.bump("s_act", 1)
        self.marks[("p2b_ogfm", ci)] = ogevv

        kvfull_ev = self.marks[("c%d" % ci, "kvfull")]
        def womm(tensor, ogevv=ogevv, kvfull_ev=kvfull_ev):
            tensor.wait_ge(sems["s_act"], ogevv)
            tensor.wait_ge(sems["s_act"], kvfull_ev)
            for k in range(KC):
                tensor.matmul(ps["pb6"][:], sb["og_fm"][:, k * 128:(k + 1) * 128],
                              sb["wo_s"][:, k * E:k * E + 512],
                              start=(k == 0), stop=(k == KC - 1))
                ins = tensor.matmul(ps["pb7"][:], sb["og_fm"][:, k * 128:(k + 1) * 128],
                                    sb["wo_s"][:, k * E + 512:(k + 1) * E],
                                    start=(k == 0), stop=(k == KC - 1))
                if k == KC - 1:
                    ins.then_inc(sems["s_pe"], 1)
        self.on("tensor", womm)
        wov = self.bump("s_pe", 1)

        prev_acc = self.marks.get(("p2b_acc", ci - 1))
        def woev(vector, wov=wov, prev_acc=prev_acc):
            v = vector
            v.wait_ge(sems["s_pe"], wov)
            if prev_acc is not None:
                v.wait_ge(sems["s_out"], prev_acc)
            v.tensor_copy(sb["oln"][:, 0:512], ps["pb6"][:])
            v.tensor_copy(sb["oln"][:, 512:1024], ps["pb7"][:]).then_inc(sems["s_dve"], 1)
        self.on("vector", woev)
        woe = self.bump("s_dve", 1)
        self.marks[("p2b_woev", ci)] = woe

        def acc(gpsimd, ci=ci, woe=woe):
            gpsimd.wait_ge(sems["s_dve"], woe)
            gpsimd.dma_start(out=self.dram["out"][ci * 128:(ci + 1) * 128, :],
                             in_=sb["oln"][:], accum_op=ALU.add).then_inc(sems["s_out"], 16)
        self.on("gpsimd", acc)
        self.marks[("p2b_acc", ci)] = self.bump("s_out", 16)

        if self.debug and ci == 0:
            def dmp(sync, otmv=otmv, ogv=ogv):
                sync.wait_ge(sems["s_dve"], ogv)
                sync.dma_start(out=self.dram["d_o"], in_=sb["o_tm"][:]).then_inc(sems["s_out"], 16)
                sync.dma_start(out=self.dram["d_oln"], in_=sb["sqs"][:]).then_inc(sems["s_out"], 16)
                sync.dma_start(out=self.dram["d_v"], in_=sb["v_grp"][:, 0:1024]).then_inc(sems["s_out"], 16)
            self.on("sync", dmp)
            self.marks["dbg_ep0"] = self.bump("s_out", 48)

    def emit_f2a_tile(self, i):
        sb, sems = self.sb, self.sems
        m, half = i // 2, i % 2
        slot = i % 2
        waits = [("s_dve", self.marks["q_sg_ready"]),
                 ("s_sp", self.marks[("w_gate", "sp", 3, IC - 1)])]
        prev = self.marks.get(("f2a", i - 2))
        if prev is not None:
            waits.append(("s_dve", prev))
        def ld(sync, m=m, half=half, slot=slot, waits=tuple(waits)):
            for s_, v_ in waits:
                sync.wait_ge(sems[s_], v_)
            sync.dma_start(out=sb["f2in"][:, slot * 1024:(slot + 1) * 1024],
                           in_=self.dram["spill_g"][m * 128:(m + 1) * 128,
                                                    half * 1024:(half + 1) * 1024]).then_inc(sems["s_sl"], 16)
        self.on("sync", ld)
        ldv = self.bump("s_sl", 16)

        def c1(vector, slot=slot, ldv=ldv):
            vector.wait_ge(sems["s_sl"], ldv)
            vector.tensor_scalar(sb["f2A"][:], sb["f2in"][:, slot * 1024:(slot + 1) * 1024],
                                 sb["q_sg"][:, 32:33], MAGIC, ALU.mult,
                                 ALU.add).then_inc(sems["s_dve"], 1)
        self.on("vector", c1)
        c1v = self.bump("s_dve", 1)

        def sg(scalar, c1v=c1v):
            scalar.wait_ge(sems["s_dve"], c1v)
            scalar.activation(sb["f2B"][:], sb["f2A"][:], AF.Sigmoid,
                              bias=sb["q_sg"][:, 48:49],
                              scale=sb["q_sg"][:, 16:17]).then_inc(sems["s_act"], 1)
        self.on("scalar", sg)
        sgv = self.bump("s_act", 1)

        prevsp = self.marks.get(("f2a_sp", i - 1))
        def c2(vector, sgv=sgv, slot=slot, base=self.cur("s_dve"), prevsp=prevsp):
            v = vector
            v.wait_ge(sems["s_act"], sgv)
            n = 0
            v.tensor_scalar(sb["f2C"][:], sb["f2B"][:], 65536.0, MAGIC, ALU.mult, ALU.add).then_inc(sems["s_dve"], 1); n += 1
            v.wait_ge(sems["s_dve"], base + n)
            v.tensor_scalar(sb["f2B"][:], sb["f2C"][:], -MAGIC, 65535.0, ALU.add, ALU.min).then_inc(sems["s_dve"], 1); n += 1
            v.wait_ge(sems["s_dve"], base + n)
            if prevsp is not None:
                v.wait_ge(sems["s_sp"], prevsp)
            v.scalar_tensor_tensor(sb["f2t"][:],
                                   sb["f2A"][:], -MAGIC, sb["f2B"][:],
                                   ALU.add, ALU.mult).then_inc(sems["s_dve"], 1); n += 1
            v.wait_ge(sems["s_dve"], base + n)
            v.reduce_max(sb["st_gmax"][:, i:i + 1],
                         sb["f2t"][:], AX,
                         apply_absolute_value=True).then_inc(sems["s_dve"], 1)
        self.on("vector", c2)
        self.bump("s_dve", 3)
        cv = self.bump("s_dve", 1)
        self.marks[("f2a", i)] = cv

        def spt(sync, m=m, half=half, slot=slot, cv=cv):
            sync.wait_ge(sems["s_dve"], cv)
            sync.dma_start(out=self.dram["spill_t"][m * 128:(m + 1) * 128,
                                                    half * 1024:(half + 1) * 1024],
                           in_=sb["f2t"][:]).then_inc(sems["s_sp"], 16)
        self.on("sync", spt)
        self.marks[("f2a_sp", i)] = self.bump("s_sp", 16)

    def p2_f2b(self):
        sb, sems = self.sb, self.sems
        last = self.marks[("f2a", 47)]
        def gm(vector, last=last):
            vector.wait_ge(sems["s_dve"], last)
            vector.reduce_max(sb["sca"][:, 3:4], sb["st_gmax"][:, 0:48], AX).then_inc(sems["s_dve"], 1)
        self.on("vector", gm)
        gmv = self.bump("s_dve", 1)
        self.ar_chain("ar3", sb["sca"][:, 3:4], [sb["q_st"][:, 0:1]], ("s_dve", gmv))
        self.scale_chain2("q_st", 65536.0 * QMAX_A, self.marks["ar3_done"])
        def coef(vector, w=self.marks["q_st_ready"], w2=self.marks["q_sg_ready"],
                 base=self.cur("s_dve")):
            v = vector
            v.wait_ge(sems["s_dve"], w)
            v.wait_ge(sems["s_dve"], w2)
            v.tensor_tensor(sb["q_st"][:, 48:64], sb["q_sg"][:, 16:32],
                            sb["q_st"][:, 32:48], ALU.mult).then_inc(sems["s_dve"], 1)
            v.wait_ge(sems["s_dve"], base + 1)
            v.tensor_scalar(sb["q_st"][:, 48:64], sb["q_st"][:, 48:64],
                            1.0 / 65536.0, None, ALU.mult).then_inc(sems["s_dve"], 1)
        self.on("vector", coef)
        self.bump("s_dve", 1)
        coefv = self.bump("s_dve", 1)

        for i in range(48):
            m, half = i // 2, i % 2
            waits = [("s_dve", coefv), ("s_dve", self.marks["q_su_ready"]),
                     ("s_sp", self.marks[("f2a_sp", 47)]),
                     ("s_sp", self.marks[("w_up", "sp", 3, IC - 1)])]
            prev = self.marks.get(("f2b", i - 1))
            if prev is not None:
                waits.append(("s_dve", prev))
            def ld(sync, m=m, half=half, waits=tuple(waits)):
                for s_, v_ in waits:
                    sync.wait_ge(sems[s_], v_)
                sync.dma_start(out=sb["f2in"][:, 0:1024],
                               in_=self.dram["spill_t"][m * 128:(m + 1) * 128,
                                                        half * 1024:(half + 1) * 1024]).then_inc(sems["s_sl"], 16)
                sync.dma_start(out=sb["f2in"][:, 1024:2048],
                               in_=self.dram["spill_u"][m * 128:(m + 1) * 128,
                                                        half * 1024:(half + 1) * 1024]).then_inc(sems["s_sl"], 16)
            self.on("sync", ld)
            ldv = self.bump("s_sl", 32)

            slot = i % 2
            prevsp = self.marks.get(("f2b_sp", i - 1))
            def ch(vector, i=i, slot=slot, ldv=ldv, base=self.cur("s_dve"), prevsp=prevsp):
                v = vector
                v.wait_ge(sems["s_sl"], ldv)
                n = 0
                v.tensor_scalar(sb["f2A"][:], sb["f2in"][:, 0:1024],
                                sb["q_st"][:, 48:49], MAGIC, ALU.mult, ALU.add).then_inc(sems["s_dve"], 1); n += 1
                v.tensor_scalar(sb["f2B"][:], sb["f2in"][:, 1024:2048],
                                sb["q_su"][:, 32:33], MAGIC, ALU.mult, ALU.add).then_inc(sems["s_dve"], 1); n += 1
                v.wait_ge(sems["s_dve"], base + n)
                v.tensor_scalar(sb["f2C"][:], sb["f2A"][:], -MAGIC, None, ALU.add).then_inc(sems["s_dve"], 1); n += 1
                v.wait_ge(sems["s_dve"], base + n)
                if prevsp is not None:
                    v.wait_ge(sems["s_sp"], prevsp)
                v.scalar_tensor_tensor(sb["f2t"][:],
                                       sb["f2B"][:], -MAGIC, sb["f2C"][:],
                                       ALU.add, ALU.mult).then_inc(sems["s_dve"], 1); n += 1
                v.wait_ge(sems["s_dve"], base + n)
                v.reduce_max(sb["st_gmax"][:, 48 + i:49 + i],
                             sb["f2t"][:], AX,
                             apply_absolute_value=True).then_inc(sems["s_dve"], 1)
            self.on("vector", ch)
            self.bump("s_dve", 4)
            cv = self.bump("s_dve", 1)
            self.marks[("f2b", i)] = cv

            def spw(sync, m=m, half=half, slot=slot, cv=cv):
                sync.wait_ge(sems["s_dve"], cv)
                sync.dma_start(out=self.dram["spill_w"][m * 128:(m + 1) * 128,
                                                        half * 1024:(half + 1) * 1024],
                               in_=sb["f2t"][:]).then_inc(sems["s_sp"], 16)
            self.on("sync", spw)
            self.marks[("f2b_sp", i)] = self.bump("s_sp", 16)

        def gm2(vector, last=self.marks[("f2b", 47)]):
            vector.wait_ge(sems["s_dve"], last)
            vector.reduce_max(sb["sca"][:, 4:5], sb["st_gmax"][:, 48:96], AX).then_inc(sems["s_dve"], 1)
        self.on("vector", gm2)
        gm2v = self.bump("s_dve", 1)
        self.ar_chain("ar4", sb["sca"][:, 4:5], [sb["q_so"][:, 0:1]], ("s_dve", gm2v))
        self.scale_chain3("q_so", self.marks["ar4_done"])

        if self.debug:
            def dmp(sync, a=self.marks[("f2a_sp", 47)], b=self.marks[("f2b_sp", 47)]):
                sync.wait_ge(sems["s_sp"], max(a, b))
                sync.dma_start(out=self.dram["d_t"],
                               in_=self.dram["spill_t"][0:128, 0:512]).then_inc(sems["s_out"], 16)
                sync.dma_start(out=self.dram["d_u2"],
                               in_=self.dram["spill_w"][0:128, 0:512]).then_inc(sems["s_out"], 16)
                sync.wait_ge(sems["s_dve"], self.marks["q_so_ready"])
                sync.dma_start(out=self.dram["d_sca4"], in_=sb["q_st"][:]).then_inc(sems["s_out"], 16)
                sync.dma_start(out=self.dram["d_sca5"], in_=sb["q_so"][:]).then_inc(sems["s_out"], 16)
            self.on("sync", dmp)
            self.bump("s_out", 64)

    def scale_chain2(self, tile, divisor, wait_mark):
        sems, sb = self.sems, self.sb
        t = sb[tile]
        def f(vector, wait_mark=wait_mark, base=self.cur("s_dve"),
              wsg=self.marks["q_sg_ready"]):
            v = vector
            v.wait_ge(sems["s_st"], wait_mark)
            v.wait_ge(sems["s_dve"], wsg)
            v.tensor_copy(t[:, 0:16], t[:, 0:1].broadcast_to((128, 16))).then_inc(sems["s_dve"], 1)
            v.wait_ge(sems["s_dve"], base + 1)
            v.tensor_tensor(t[:, 0:16], t[:, 0:16], sb["q_sg"][:, 16:32], ALU.mult).then_inc(sems["s_dve"], 1)
            v.wait_ge(sems["s_dve"], base + 2)
            v.tensor_scalar(t[:, 16:32], t[:, 0:16], 1.0 / divisor, 1e-12, ALU.mult, ALU.max).then_inc(sems["s_dve"], 1)
            v.wait_ge(sems["s_dve"], base + 3)
            v.reciprocal(t[:, 32:48], t[:, 16:32]).then_inc(sems["s_dve"], 1)
        self.on("vector", f)
        self.bump("s_dve", 3)
        self.marks[tile + "_ready"] = self.bump("s_dve", 1)

    def scale_chain3(self, tile, wait_mark):
        sems, sb = self.sems, self.sb
        t = sb[tile]
        def f(vector, wait_mark=wait_mark, base=self.cur("s_dve"),
              wst=self.marks["q_st_ready"], wsu=self.marks["q_su_ready"]):
            v = vector
            v.wait_ge(sems["s_st"], wait_mark)
            v.wait_ge(sems["s_dve"], wst)
            v.wait_ge(sems["s_dve"], wsu)
            n = 0
            v.tensor_copy(t[:, 0:16], t[:, 0:1].broadcast_to((128, 16))).then_inc(sems["s_dve"], 1); n += 1
            v.wait_ge(sems["s_dve"], base + n)
            v.tensor_tensor(t[:, 48:64], sb["q_st"][:, 16:32], sb["q_su"][:, 16:32], ALU.mult).then_inc(sems["s_dve"], 1); n += 1
            v.wait_ge(sems["s_dve"], base + n)
            v.tensor_tensor(t[:, 0:16], t[:, 0:16], t[:, 48:64], ALU.mult).then_inc(sems["s_dve"], 1); n += 1
            v.wait_ge(sems["s_dve"], base + n)
            v.tensor_scalar(t[:, 16:32], t[:, 0:16], 1.0 / QMAX_A, 1e-12, ALU.mult, ALU.max).then_inc(sems["s_dve"], 1); n += 1
            v.wait_ge(sems["s_dve"], base + n)
            v.reciprocal(t[:, 32:48], t[:, 16:32]).then_inc(sems["s_dve"], 1); n += 1
            v.wait_ge(sems["s_dve"], base + n)
            v.tensor_tensor(t[:, 32:48], t[:, 48:64], t[:, 32:48], ALU.mult).then_inc(sems["s_dve"], 1); n += 1
        self.on("vector", f)
        self.bump("s_dve", 5)
        self.marks[tile + "_ready"] = self.bump("s_dve", 1)

    def p3(self):
        S, sb, ps, sems = self.S, self.sb, self.ps, self.sems
        S("wd_s", (128, IC * E), BF16)
        S("ro_raw", (128, 2 * I), BF16)
        S("ro_y", (128, I), F32)
        S("ro_bf", (128, 2 * I), BF16)
        S("d_ev", (128, 2 * E), F32)
        p2done = self.marks["p2_all"]
        def wdld(sync, a=p2done["s_dve"], b=p2done["s_act"], c=p2done["s_pe"],
                 d=p2done["s_out"], e=p2done["s_sp"], f=p2done.get("s_sl", 0)):
            sync.wait_ge(sems["s_dve"], a)
            sync.wait_ge(sems["s_act"], b)
            sync.wait_ge(sems["s_pe"], c)
            sync.wait_ge(sems["s_out"], d)
            sync.wait_ge(sems["s_sp"], e)
            sync.wait_ge(sems["s_sl"], f)
            sync.dma_start(out=sb["wd_s"][:], in_=self.dram["spill_wd"]).then_inc(sems["s_w"], 16)
        self.on("sync", wdld)
        wdv = self.bump("s_w", 16)

        sor = self.marks["q_so_ready"]
        for ci in range(C):
            slot = ci % 2
            waits = [("s_w", wdv), ("s_dve", sor),
                     ("s_sp", self.marks[("f2b_sp", 47)])]
            prev = self.marks.get(("p3_ro", ci - 2))
            if prev is not None:
                waits.append(("s_dve", prev))
            def ld(sync, ci=ci, slot=slot, waits=tuple(waits)):
                for s_, v_ in waits:
                    sync.wait_ge(sems[s_], v_)
                src = self.dram["spill_w"].rearrange("(k p) t -> p k t", p=128)[
                    :, :, ci * 128:(ci + 1) * 128]
                sync.dma_start(out=sb["ro_raw"][:, slot * I:(slot + 1) * I].rearrange(
                    "p (k t) -> p k t", t=128), in_=src).then_inc(sems["s_sl"], 16)
            self.on("sync", ld)
            ldv = self.bump("s_sl", 16)

            robf_prev = self.marks.get(("p3_mm", ci - 2))
            def ch(vector, ci=ci, slot=slot, ldv=ldv, base=self.cur("s_dve"),
                   robf_prev=robf_prev):
                v = vector
                v.wait_ge(sems["s_sl"], ldv)
                if robf_prev is not None:
                    v.wait_ge(sems["s_pe"], robf_prev)
                v.tensor_scalar(sb["ro_y"][:], sb["ro_raw"][:, slot * I:(slot + 1) * I],
                                sb["q_so"][:, 32:33], MAGIC, ALU.mult, ALU.add).then_inc(sems["s_dve"], 1)
                v.wait_ge(sems["s_dve"], base + 1)
                v.tensor_scalar(sb["ro_bf"][:, slot * I:(slot + 1) * I], sb["ro_y"][:],
                                -MAGIC, None, ALU.add).then_inc(sems["s_dve"], 1)
            self.on("vector", ch)
            self.bump("s_dve", 1)
            chv = self.bump("s_dve", 1)
            self.marks[("p3_ro", ci)] = chv

            bA = "pb%d" % (0 + 2 * (ci % 2))
            bB = "pb%d" % (1 + 2 * (ci % 2))
            mw = [("s_dve", chv)]
            prevev = self.marks.get(("p3_ev", ci - 2))
            if prevev is not None:
                mw.append(("s_dve", prevev))
            def mm(tensor, ci=ci, slot=slot, bA=bA, bB=bB, mw=tuple(mw)):
                for s_, v_ in mw:
                    tensor.wait_ge(sems[s_], v_)
                for k in range(IC):
                    tensor.matmul(ps[bA][:],
                                  sb["ro_bf"][:, slot * I + k * 128:slot * I + (k + 1) * 128],
                                  sb["wd_s"][:, k * E:k * E + 512],
                                  start=(k == 0), stop=(k == IC - 1))
                    ins = tensor.matmul(ps[bB][:],
                                        sb["ro_bf"][:, slot * I + k * 128:slot * I + (k + 1) * 128],
                                        sb["wd_s"][:, k * E + 512:(k + 1) * E],
                                        start=(k == 0), stop=(k == IC - 1))
                    if k == IC - 1:
                        ins.then_inc(sems["s_pe"], 1)
            self.on("tensor", mm)
            mmv = self.bump("s_pe", 1)
            self.marks[("p3_mm", ci)] = mmv

            prev_acc = self.marks.get(("p3_acc", ci - 2))
            def ev(vector, ci=ci, slot=slot, bA=bA, bB=bB, mmv=mmv,
                   prev_acc=prev_acc, base=self.cur("s_dve")):
                v = vector
                v.wait_ge(sems["s_pe"], mmv)
                if prev_acc is not None:
                    v.wait_ge(sems["s_out"], prev_acc)
                v.tensor_scalar(sb["d_ev"][:, slot * E:slot * E + 512], ps[bA][:],
                                sb["q_so"][:, 16:17], None, ALU.mult).then_inc(sems["s_dve"], 1)
                v.tensor_scalar(sb["d_ev"][:, slot * E + 512:(slot + 1) * E], ps[bB][:],
                                sb["q_so"][:, 16:17], None, ALU.mult).then_inc(sems["s_dve"], 1)
            self.on("vector", ev)
            self.bump("s_dve", 1)
            evv = self.bump("s_dve", 1)
            self.marks[("p3_ev", ci)] = evv

            def acc(gpsimd, ci=ci, slot=slot, evv=evv):
                gpsimd.wait_ge(sems["s_dve"], evv)
                gpsimd.dma_start(out=self.dram["out"][ci * 128:(ci + 1) * 128, :],
                                 in_=sb["d_ev"][:, slot * E:(slot + 1) * E],
                                 accum_op=ALU.add).then_inc(sems["s_out"], 16)
            self.on("gpsimd", acc)
            self.marks[("p3_acc", ci)] = self.bump("s_out", 16)


def build_nc(stage=0, debug=False):
    return K(stage, debug).build()


def make_in_maps(inputs):
    x = np.ascontiguousarray(np.asarray(inputs["L0_attn_x"], np.float32))
    gam = np.ascontiguousarray(np.asarray(inputs["L0_attn_gamma"], np.float32).reshape(KC, 128).T)
    bet = np.ascontiguousarray(np.asarray(inputs["L0_attn_beta"], np.float32).reshape(KC, 128).T)
    weights = dict(
        wq=np.ascontiguousarray(np.asarray(inputs["L0_attn_Wq"], np.float32)),
        wk=np.ascontiguousarray(np.asarray(inputs["L0_attn_Wk"], np.float32)),
        wv=np.ascontiguousarray(np.asarray(inputs["L0_attn_Wv"], np.float32)),
        wo=np.ascontiguousarray(np.asarray(inputs["L0_attn_Wo"], np.float32)),
        wg1=np.ascontiguousarray(np.asarray(inputs["L0_attn_Wg1"], np.float32)),
        wg2=np.ascontiguousarray(np.asarray(inputs["L0_attn_Wg2"], np.float32)),
        w_up=np.ascontiguousarray(np.asarray(inputs["L0_ffn_w_up"], np.float32)),
        w_gate=np.ascontiguousarray(np.asarray(inputs["L0_ffn_w_gate"], np.float32)),
        w_down=np.ascontiguousarray(np.asarray(inputs["L0_ffn_w_down"], np.float32)),
        gamma_fm=gam, beta_fm=bet,
        gamma_row=np.ascontiguousarray(np.asarray(inputs["L0_attn_gamma"], np.float32).reshape(1, E)),
        beta_row=np.ascontiguousarray(np.asarray(inputs["L0_attn_beta"], np.float32).reshape(1, E)),
    )
    in_maps = []
    for core in range(N_CORES):
        b, seg = core // 4, core % 4
        m = dict(weights)
        m["x"] = np.ascontiguousarray(x[b, seg * T:(seg + 1) * T])
        m.update(_host_consts(core))
        in_maps.append(m)
    return in_maps


def kernel(L0_attn_x, L0_attn_Wq, L0_attn_Wk, L0_attn_Wv, L0_attn_Wo,
           L0_attn_gamma, L0_attn_beta, L0_attn_Wg1, L0_attn_Wg2,
           L0_ffn_w_gate, L0_ffn_w_up, L0_ffn_w_down):
    inputs = dict(L0_attn_x=L0_attn_x, L0_attn_Wq=L0_attn_Wq, L0_attn_Wk=L0_attn_Wk,
                  L0_attn_Wv=L0_attn_Wv, L0_attn_Wo=L0_attn_Wo, L0_attn_gamma=L0_attn_gamma,
                  L0_attn_beta=L0_attn_beta, L0_attn_Wg1=L0_attn_Wg1, L0_attn_Wg2=L0_attn_Wg2,
                  L0_ffn_w_gate=L0_ffn_w_gate, L0_ffn_w_up=L0_ffn_w_up,
                  L0_ffn_w_down=L0_ffn_w_down)
    stage = int(os.environ.get("KSTAGE", "3"))
    debug = bool(int(os.environ.get("KDEBUG", "0")))
    nc = build_nc(stage=stage, debug=debug)
    in_maps = make_in_maps(inputs)
    res = run_bass_kernel_spmd(nc, in_maps, core_ids=list(range(N_CORES)))
    out = np.empty((B, N, E), np.float32)
    for core in range(N_CORES):
        b, seg = core // 4, core % 4
        out[b, seg * T:(seg + 1) * T] = np.asarray(res.results[core]["out"])
    if debug:
        kernel.last_results = res
    return out
